# version 06
# baseline (speedup 1.0000x reference)
import numpy as np
from contextlib import ExitStack

import concourse.bass as bass
import concourse.mybir as mybir
from concourse.bass_utils import run_bass_kernel_spmd

F32 = mybir.dt.float32
BF16 = mybir.dt.bfloat16
AF = mybir.ActivationFunctionType
ALU = mybir.AluOpType
AX = mybir.AxisListType.X

P = 128
D = 1024
ND = D // P
FF = 32768
NFC = FF // P
FCB = 4
NFCB = NFC // FCB
TS_T = 384
QB = 1024
KC = 1024
G1 = 512
NTC = G1 // P
EPS = 1e-5
SCALE = 1.0 / 32.0


class Sem:
    def __init__(self, nc, name):
        self.h = nc.alloc_semaphore(name)
        self.n = 0

    def inc(self, ins, k=1):
        ins.then_inc(self.h, k)
        self.n += k


class DmaStream:

    def __init__(self, nc, name, nbufs=2):
        self.nb = nbufs
        self.s = [Sem(nc, name + str(i)) for i in range(nbufs)]
        self.cum = {}

    def inc(self, ins, i):
        s = self.s[i % self.nb]
        s.inc(ins, 16)
        self.cum[i] = s.n

    def wait_done(self, eng, i):
        eng.wait_ge(self.s[i % self.nb].h, self.cum[i])

    def wait_count(self, eng, i, per_index):
        eng.wait_ge(self.s[i % self.nb].h, 16 * per_index * (i // self.nb + 1))

    def wait_all(self, eng):
        for s in self.s:
            if s.n:
                eng.wait_ge(s.h, s.n)


def build(S=8192):
    nc = bass.Bass("TRN2", target_bir_lowering=False)

    x_d = nc.dram_tensor("x", [S, D], F32, kind="ExternalInput").ap()
    qw_d = nc.dram_tensor("qw", [D, D], F32, kind="ExternalInput").ap()
    kw_d = nc.dram_tensor("kw", [D, D], F32, kind="ExternalInput").ap()
    vw_d = nc.dram_tensor("vw", [D, D], F32, kind="ExternalInput").ap()
    wg_d = nc.dram_tensor("wg", [8, D, 4096], F32, kind="ExternalInput").ap()
    wu_d = nc.dram_tensor("wu", [8, D, 4096], F32, kind="ExternalInput").ap()
    wd_d = nc.dram_tensor("wd", [8, 4096, D], F32, kind="ExternalInput").ap()
    out_d = nc.dram_tensor("out", [S, D], F32, kind="ExternalOutput").ap()

    glp = nc.dram_tensor("glp", [D, FF], BF16, kind="Internal").ap()
    ulp = nc.dram_tensor("ulp", [D, FF], BF16, kind="Internal").ap()
    dlp = nc.dram_tensor("dlp", [FF, D], BF16, kind="Internal").ap()
    qfm = nc.dram_tensor("qfm", [D, S], BF16, kind="Internal").ap()
    kfm = nc.dram_tensor("kfm", [D, S], BF16, kind="Internal").ap()
    vtok = nc.dram_tensor("vtok", [S, D], BF16, kind="Internal").ap()
    h1tok = nc.dram_tensor("h1tok", [S, D], F32, kind="Internal").ap()
    h1fm = nc.dram_tensor("h1fm", [D, S], BF16, kind="Internal").ap()

    NG = S // G1
    NQB = S // QB
    ts_list = []
    t0 = 0
    while t0 < S:
        t = min(TS_T, S - t0)
        ts_list.append((t0, t))
        t0 += t
    NTS = len(ts_list)

    def gl_of(qb, kc):
        return qb * (qb + 1) // 2 + kc

    pairs = []
    ndiag = 0
    for qb in range(NQB):
        for kc in range(qb + 1):
            for qs in range(8):
                diag = kc == qb
                if diag:
                    nv = (qs + 1) * P
                    nh = (nv + 511) // 512
                    nksc = qs + 1
                    di = ndiag
                    ndiag += 1
                else:
                    nh, nksc, di = 2, 8, None
                pairs.append(dict(qb=qb, kc=kc, qs=qs, gl=gl_of(qb, kc),
                                  gp=len(pairs), diag=diag, nh=nh,
                                  nksc=nksc, W=nh * 512, di=di,
                                  first=(kc == 0)))
    NPAIR = len(pairs)
    cum_sch = [0]
    for pr in pairs:
        cum_sch.append(cum_sch[-1] + (1 if pr["nh"] == 2 else 0))

    with ExitStack() as top:
        sm = {k: Sem(nc, k) for k in [
            "wcast", "qkvw", "qmm", "kmm", "vmm", "qev", "kev", "vev",
            "sc", "dve1", "expd", "msk", "av", "oup", "fdve", "fact", "fin",
            "gmm", "umm", "silu", "mul", "ydone", "h2dve", "h2act", "h2ev",
            "xc", "sch",
        ]}
        st = {k: DmaStream(nc, k) for k in [
            "xlp", "xtr", "qst", "kst", "vst",
            "qld", "kld", "vld", "xres", "tokst", "h1tr", "h1st",
            "wgld", "wuld", "wdld", "h1ld", "resld",
        ]}
        st["pt"] = DmaStream(nc, "pt", nbufs=3)
        st["outst"] = DmaStream(nc, "outst", nbufs=4)
        block = top.enter_context(nc.Block())

        with ExitStack() as p1:
            e1 = p1.enter_context
            qw_sb = e1(nc.sbuf_tensor("qw_sb", [P, ND * D], BF16))
            kw_sb = e1(nc.sbuf_tensor("kw_sb", [P, ND * D], BF16))
            vw_sb = e1(nc.sbuf_tensor("vw_sb", [P, ND * D], BF16))
            x_lp = e1(nc.sbuf_tensor("x_lp", [P, 2 * NTC * D], BF16))
            x_f32 = e1(nc.sbuf_tensor("x_f32", [P, 2 * NTC * D], F32))
            x_fm = e1(nc.sbuf_tensor("x_fm", [P, 2 * ND * G1], BF16))
            q_stg = e1(nc.sbuf_tensor("q_stg", [P, 2 * 8 * 512], BF16))
            k_stg = e1(nc.sbuf_tensor("k_stg", [P, 2 * 8 * 512], BF16))
            v_stg = e1(nc.sbuf_tensor("v_stg", [P, 2 * 8 * 512], BF16))
            q_ps = [e1(nc.psum_tensor(f"q_ps{i}", [P, 512], F32)) for i in range(2)]
            k_ps = [e1(nc.psum_tensor(f"k_ps{i}", [P, 512], F32)) for i in range(2)]
            v_ps = [e1(nc.psum_tensor(f"v_ps{i}", [P, 512], F32)) for i in range(2)]

            def xfm(par, dc, lo, n):
                base = par * ND * G1 + dc * G1
                return x_fm[:, base + lo:base + lo + n]

            @block.gpsimd
            def _(g):
                for w_sb, w_dd in ((qw_sb, qw_d), (kw_sb, kw_d), (vw_sb, vw_d)):
                    ins = g.dma_start(
                        w_sb[:, :].rearrange("p (c o) -> p c o", c=ND),
                        w_dd.rearrange("(c p) o -> p c o", p=P))
                    sm["qkvw"].inc(ins, 16)

            QKVW_ALL = sm["qkvw"].n
            WCAST_ALL = 16 * 48

            def p1_stores(s, gi):
                par = gi % 2
                for stg, ev, store_st, dst in (
                        (q_stg, sm["qev"], st["qst"], qfm),
                        (k_stg, sm["kev"], st["kst"], kfm)):
                    s.wait_ge(ev.h, 8 * (gi + 1))
                    ins = s.dma_start(
                        dst.rearrange("(oc p) s -> p oc s", p=P)[
                            :, :, gi * G1:(gi + 1) * G1],
                        stg[:, par * 8 * 512:(par + 1) * 8 * 512]
                        .rearrange("p (oc s) -> p oc s", oc=ND))
                    store_st.inc(ins, gi)
                s.wait_ge(sm["vev"].h, 8 * (gi + 1))
                ins = s.dma_start(
                    vtok[gi * G1:(gi + 1) * G1, :]
                    .rearrange("(tc p) d -> p tc d", p=P),
                    v_stg[:, par * 8 * 512:(par + 1) * 8 * 512]
                    .rearrange("p (tc d) -> p tc d", tc=NTC))
                st["vst"].inc(ins, gi)

            def x_load(s, gi):
                par = gi % 2
                if gi >= 2:
                    s.wait_ge(sm["xc"].h, gi - 1)
                ins = s.dma_start(
                    x_f32[:, par * NTC * D:(par + 1) * NTC * D]
                    .rearrange("p (tc d) -> p tc d", tc=NTC),
                    x_d.rearrange("(tc p) d -> tc p d", p=P)[
                        gi * NTC:(gi + 1) * NTC].rearrange("tc p d -> p tc d"))
                st["xlp"].inc(ins, gi)

            @block.sync
            def _(s):
                x_load(s, 0)
                for gi in range(NG):
                    par = gi % 2
                    if gi + 1 < NG:
                        x_load(s, gi + 1)
                    s.wait_ge(sm["xc"].h, gi + 1)
                    if gi >= 2:
                        s.wait_ge(sm["vmm"].h, 8 * (gi - 1))
                    for tc in range(NTC):
                        dst = x_fm[:, par * ND * G1:(par + 1) * ND * G1] \
                            .rearrange("p (dc g) -> p dc g", dc=ND)[
                                :, :, tc * P:(tc + 1) * P]
                        srcv = x_lp[:, (par * NTC + tc) * D:
                                    (par * NTC + tc + 1) * D] \
                            .rearrange("p (dc c) -> p dc c", dc=ND)
                        ins = s.dma_start(dst, srcv, transpose=True)
                        st["xtr"].inc(ins, gi)
                    if gi >= 1:
                        p1_stores(s, gi - 1)
                p1_stores(s, NG - 1)

            @block.tensor
            def _(t):
                t.wait_ge(sm["qkvw"].h, QKVW_ALL)
                for gi in range(NG):
                    par = gi % 2
                    st["xtr"].wait_done(t, gi)
                    for w_sb, ps, mm, ev in ((qw_sb, q_ps, sm["qmm"], sm["qev"]),
                                             (kw_sb, k_ps, sm["kmm"], sm["kev"])):
                        for oc in range(ND):
                            j = gi * ND + oc
                            if j >= 2:
                                t.wait_ge(ev.h, j - 1)
                            for dc in range(ND):
                                ins = t.matmul(
                                    ps[j % 2][:, :],
                                    w_sb[:, dc * D + oc * P:dc * D + (oc + 1) * P],
                                    xfm(par, dc, 0, G1),
                                    start=(dc == 0), stop=(dc == ND - 1))
                            mm.inc(ins, 1)
                    for tc in range(NTC):
                        for dh in range(2):
                            j = gi * ND + tc * 2 + dh
                            if j >= 2:
                                t.wait_ge(sm["vev"].h, j - 1)
                            for dc in range(ND):
                                ins = t.matmul(
                                    v_ps[j % 2][:, :],
                                    xfm(par, dc, tc * P, P),
                                    vw_sb[:, dc * D + dh * 512:dc * D + (dh + 1) * 512],
                                    start=(dc == 0), stop=(dc == ND - 1))
                            sm["vmm"].inc(ins, 1)

            @block.scalar
            def _(sc):
                for gi in range(NG):
                    par = gi % 2
                    for ps, stg, mm, ev, store_st in (
                            (q_ps, q_stg, sm["qmm"], sm["qev"], st["qst"]),
                            (k_ps, k_stg, sm["kmm"], sm["kev"], st["kst"])):
                        if gi >= 2:
                            store_st.wait_done(sc, gi - 2)
                        for oc in range(ND):
                            j = gi * ND + oc
                            sc.wait_ge(mm.h, j + 1)
                            ins = sc.activation(
                                stg[:, (par * 8 + oc) * 512:(par * 8 + oc + 1) * 512],
                                ps[j % 2][:, :], AF.Copy)
                            ev.inc(ins, 1)

            @block.vector
            def _(v):
                for gi in range(NG):
                    par = gi % 2
                    st["xlp"].wait_done(v, gi)
                    if gi >= 2:
                        st["xtr"].wait_done(v, gi - 2)
                    ins = v.tensor_copy(
                        x_lp[:, par * NTC * D:(par + 1) * NTC * D],
                        x_f32[:, par * NTC * D:(par + 1) * NTC * D])
                    sm["xc"].inc(ins, 1)
                    if gi >= 2:
                        st["vst"].wait_done(v, gi - 2)
                    for tc in range(NTC):
                        for dh in range(2):
                            j = gi * ND + tc * 2 + dh
                            v.wait_ge(sm["vmm"].h, j + 1)
                            ins = v.tensor_copy(
                                v_stg[:, (par * 8 + tc * 2 + dh) * 512:
                                      (par * 8 + tc * 2 + dh + 1) * 512],
                                v_ps[j % 2][:, :])
                            sm["vev"].inc(ins, 1)

        with ExitStack() as p2:
            e2 = p2.enter_context
            q_sb = e2(nc.sbuf_tensor("q_sb", [P, 2 * ND * QB], BF16))
            k_sb = e2(nc.sbuf_tensor("k_sb", [P, 2 * ND * KC], BF16))
            v_sb = e2(nc.sbuf_tensor("v_sb", [P, 2 * (KC // P) * D], BF16))
            o_acc = e2(nc.sbuf_tensor("o_acc", [P, 8 * D], F32))
            p_sb = e2(nc.sbuf_tensor("p_sb", [P, 3 * KC], BF16))
            pt_sb = e2(nc.sbuf_tensor("pt_sb", [P, 3 * KC], BF16))
            m_sb = e2(nc.sbuf_tensor("m_sb", [P, 8], F32))
            l_sb = e2(nc.sbuf_tensor("l_sb", [P, 8], F32))
            sml = e2(nc.sbuf_tensor("sml", [P, 24], F32))
            x_res = e2(nc.sbuf_tensor("x_res", [P, 2 * D], F32))
            h_pre = e2(nc.sbuf_tensor("h_pre", [P, D], F32))
            scr = e2(nc.sbuf_tensor("scr", [P, D], F32))
            scr_a = e2(nc.sbuf_tensor("scr_a", [P, D], F32))
            h1f32 = e2(nc.sbuf_tensor("h1f32", [P, 2 * D], F32))
            h1lp = e2(nc.sbuf_tensor("h1lp", [P, 2 * D], BF16))
            h1fmt = e2(nc.sbuf_tensor("h1fmt", [P, 2 * D], BF16))
            s_ps = [e2(nc.psum_tensor(f"s_ps{i}", [P, KC], F32)) for i in range(2)]
            av_ps = [e2(nc.psum_tensor(f"av_ps{i}", [P, D], F32)) for i in range(2)]

            def qsl(qb, dc, qs):
                base = (qb % 2) * ND * QB + dc * QB
                return q_sb[:, base + qs * P:base + (qs + 1) * P]

            def ksl(gl, dc, lo, n):
                base = (gl % 2) * ND * KC + dc * KC
                return k_sb[:, base + lo:base + lo + n]

            def vsl(gl, ksc, lo, n):
                base = (gl % 2) * (KC // P) * D + ksc * D
                return v_sb[:, base + lo:base + lo + n]

            @block.sync
            def _(s):
                st["qst"].wait_all(s)
                st["kst"].wait_all(s)
                st["vst"].wait_all(s)
                def kv_load(qb, kc):
                    gl = gl_of(qb, kc)
                    if gl >= 2:
                        s.wait_ge(sm["sc"].h, 8 * (gl - 1))
                    ins = s.dma_start(
                        k_sb[:, (gl % 2) * ND * KC:((gl % 2) + 1) * ND * KC]
                        .rearrange("p (c k) -> p c k", c=ND),
                        kfm.rearrange("(c p) k -> p c k", p=P)[
                            :, :, kc * KC:(kc + 1) * KC])
                    st["kld"].inc(ins, gl)
                    if gl >= 2:
                        s.wait_ge(sm["av"].h, 8 * (gl - 1))
                    ins = s.dma_start(
                        v_sb[:, (gl % 2) * (KC // P) * D:((gl % 2) + 1) * (KC // P) * D]
                        .rearrange("p (k d) -> p k d", k=KC // P),
                        vtok.rearrange("(k p) d -> k p d", p=P)[
                            kc * (KC // P):(kc + 1) * (KC // P)]
                        .rearrange("k p d -> p k d"))
                    st["vld"].inc(ins, gl)

                def fin_dmas(s, qb, qs):
                    fq = qb * 8 + qs
                    par = fq % 2
                    row = qb * QB + qs * P
                    if fq >= 2:
                        s.wait_ge(sm["fdve"].h, 3 * (fq - 2) + 1)
                    ins = s.dma_start(x_res[:, par * D:(par + 1) * D],
                                      x_d[row:row + P, :])
                    st["xres"].inc(ins, fq)
                    s.wait_ge(sm["fdve"].h, 3 * fq + 3)
                    ins = s.dma_start(h1tok[row:row + P, :],
                                      h1f32[:, par * D:(par + 1) * D])
                    st["tokst"].inc(ins, fq)
                    s.wait_ge(sm["fin"].h, fq + 1)
                    if fq >= 2:
                        st["h1st"].wait_done(s, fq - 2)
                    ins = s.dma_start(
                        h1fmt[:, par * D:(par + 1) * D]
                        .rearrange("p (dc c) -> p dc c", dc=ND),
                        h1lp[:, par * D:(par + 1) * D]
                        .rearrange("p (dc c) -> p dc c", dc=ND),
                        transpose=True)
                    st["h1tr"].inc(ins, fq)
                    st["h1tr"].wait_done(s, fq)
                    ins = s.dma_start(
                        h1fm.rearrange("(c p) t -> p c t", p=P)[:, :, row:row + P],
                        h1fmt[:, par * D:(par + 1) * D]
                        .rearrange("p (c t) -> p c t", c=ND))
                    st["h1st"].inc(ins, fq)

                for qb in range(NQB):
                    if qb == 0:
                        ins = s.dma_start(
                            q_sb[:, 0:ND * QB].rearrange("p (c q) -> p c q", c=ND),
                            qfm.rearrange("(c p) q -> p c q", p=P)[:, :, 0:QB])
                        st["qld"].inc(ins, 0)
                        kv_load(0, 0)
                    for kc in range(qb + 1):
                        gl = gl_of(qb, kc)
                        if kc + 1 <= qb:
                            kv_load(qb, kc + 1)
                        for qs in range(8):
                            if kc == 0 and qb >= 1:
                                fin_dmas(s, qb - 1, qs)
                            pr = pairs[gl * 8 + qs]
                            gp = pr["gp"]
                            par3 = gp % 3
                            s.wait_ge(sm["expd"].h, gp + 1)
                            if pr["diag"]:
                                s.wait_ge(sm["msk"].h, pr["di"] + 1)
                            if gp >= 3:
                                s.wait_ge(sm["av"].h, gp - 2)
                            nk = pr["nksc"]
                            ins = s.dma_start(
                                pt_sb[:, par3 * KC:par3 * KC + nk * P]
                                .rearrange("p (k c) -> p k c", k=nk),
                                p_sb[:, par3 * KC:par3 * KC + nk * P]
                                .rearrange("p (k c) -> p k c", k=nk),
                                transpose=True)
                            st["pt"].inc(ins, gp)
                    if qb + 1 < NQB:
                        if qb >= 1:
                            s.wait_ge(sm["sc"].h, 8 * (gl_of(qb - 1, qb - 1) + 1))
                        ins = s.dma_start(
                            q_sb[:, ((qb + 1) % 2) * ND * QB:((qb + 1) % 2 + 1) * ND * QB]
                            .rearrange("p (c q) -> p c q", c=ND),
                            qfm.rearrange("(c p) q -> p c q", p=P)[
                                :, :, (qb + 1) * QB:(qb + 2) * QB])
                        st["qld"].inc(ins, qb + 1)
                        kv_load(qb + 1, 0)
                for qs in range(8):
                    fin_dmas(s, NQB - 1, qs)

            @block.tensor
            def _(t):
                def emit_av(pr):
                    gp = pr["gp"]
                    par = gp % 2
                    par3 = gp % 3
                    if pr["qs"] == 0:
                        st["vld"].wait_done(t, pr["gl"])
                    st["pt"].wait_done(t, gp)
                    if gp >= 2:
                        t.wait_ge(sm["oup"].h, gp - 1)
                    for dh in range(2):
                        for ksc in range(pr["nksc"]):
                            ins = t.matmul(
                                av_ps[par][:, dh * 512:(dh + 1) * 512],
                                pt_sb[:, par3 * KC + ksc * P:par3 * KC + (ksc + 1) * P],
                                vsl(pr["gl"], ksc, dh * 512, 512),
                                start=(ksc == 0), stop=(ksc == pr["nksc"] - 1))
                    sm["av"].inc(ins, 1)

                for pr in pairs:
                    gp = pr["gp"]
                    par = gp % 2
                    if pr["qs"] == 0:
                        st["kld"].wait_done(t, pr["gl"])
                        if pr["kc"] == 0:
                            st["qld"].wait_done(t, pr["qb"])
                    if gp >= 2:
                        t.wait_ge(sm["expd"].h, gp - 1)
                    for nh in range(pr["nh"]):
                        for dc in range(ND):
                            ins = t.matmul(
                                s_ps[par][:, nh * 512:(nh + 1) * 512],
                                qsl(pr["qb"], dc, pr["qs"]),
                                ksl(pr["gl"], dc, nh * 512, 512),
                                start=(dc == 0), stop=(dc == ND - 1))
                        if nh == 0 and pr["nh"] == 2:
                            sm["sch"].inc(ins, 1)
                    sm["sc"].inc(ins, 1)
                    if gp >= 2 and pairs[gp - 2]["qb"] == pr["qb"]:
                        emit_av(pairs[gp - 2])
                    if pr["kc"] == pr["qb"] and pr["qs"] == 7:
                        if gp >= 1 and pairs[gp - 1]["qb"] == pr["qb"]:
                            emit_av(pairs[gp - 1])
                        emit_av(pr)

            @block.gpsimd
            def _(g):
                chunks = []
                for e in range(8):
                    for h in range(2):
                        chunks.append((glp[:, e * 4096 + h * 2048:e * 4096 + (h + 1) * 2048],
                                       wg_d[e][:, h * 2048:(h + 1) * 2048]))
                        chunks.append((ulp[:, e * 4096 + h * 2048:e * 4096 + (h + 1) * 2048],
                                       wu_d[e][:, h * 2048:(h + 1) * 2048]))
                        chunks.append((dlp[e * 4096 + h * 2048:e * 4096 + (h + 1) * 2048, :],
                                       wd_d[e][h * 2048:(h + 1) * 2048, :]))
                NCH = len(chunks)
                assert 16 * NCH == WCAST_ALL
                n_p1 = 8
                items = []
                for i in range(NCH):
                    if i < n_p1:
                        items.append((-1000 + i, "cast", i))
                    else:
                        frac = (i - n_p1) / max(1, NCH - n_p1 - 1)
                        items.append((int(frac * (NPAIR - 16)), "cast", i))
                for pr in pairs:
                    if pr["diag"]:
                        items.append((pr["gp"] + 1, "mask", pr))
                items.sort(key=lambda x: (x[0], x[1] == "cast"))
                for key, kind, payload in items:
                    if kind == "cast":
                        i = payload
                        if i < n_p1:
                            g.wait_ge(sm["vmm"].h, min(8 * NG, 16 * i))
                        else:
                            g.wait_ge(sm["sc"].h, min(NPAIR, max(0, key)))
                        ins = g.dma_start(*chunks[i])
                        sm["wcast"].inc(ins, 16)
                    else:
                        pr = payload
                        gp, qs = pr["gp"], pr["qs"]
                        par3 = gp % 3
                        g.wait_ge(sm["expd"].h, gp + 1)
                        ins = g.affine_select(
                            p_sb[:, par3 * KC:par3 * KC + pr["W"]],
                            p_sb[:, par3 * KC:par3 * KC + pr["W"]],
                            [[-1, pr["W"]]], ALU.is_ge, 0.0,
                            base=qs * P, channel_multiplier=1)
                        sm["msk"].inc(ins, 1)

            def dve_finalize(v, qb, qs):
                fq = qb * 8 + qs
                par = fq % 2
                st["xres"].wait_done(v, fq)
                v.drain()
                v.reciprocal(sml[:, 22:23], l_sb[:, qs:qs + 1])
                v.drain()
                ins = v.scalar_tensor_tensor(
                    h_pre[:, :], o_acc[:, qs * D:(qs + 1) * D],
                    sml[:, 22:23], x_res[:, par * D:(par + 1) * D],
                    ALU.mult, ALU.add)
                sm["fdve"].inc(ins, 1)
                v.drain()
                v.scalar_tensor_tensor(
                    scr[:, :], h_pre[:, :], 1.0, h_pre[:, :],
                    ALU.mult, ALU.mult, accum_out=sml[:, 16:17])
                v.wait_ge(sm["fact"].h, 2 * fq + 1)
                v.tensor_scalar_mul(sml[:, 18:19], sml[:, 17:18], -1.0 / D)
                v.drain()
                v.tensor_tensor(sml[:, 19:20], sml[:, 18:19],
                                sml[:, 18:19], ALU.mult)
                v.drain()
                v.scalar_tensor_tensor(
                    sml[:, 19:20], sml[:, 16:17], 1.0 / D,
                    sml[:, 19:20], ALU.mult, ALU.subtract)
                v.drain()
                ins = v.tensor_scalar_add(sml[:, 19:20],
                                          sml[:, 19:20], EPS)
                sm["fdve"].inc(ins, 1)
                v.wait_ge(sm["fact"].h, 2 * fq + 2)
                if fq >= 2:
                    st["tokst"].wait_done(v, fq - 2)
                v.drain()
                ins = v.tensor_scalar(h1f32[:, par * D:(par + 1) * D],
                                      h_pre[:, :], sml[:, 18:19],
                                      sml[:, 20:21], ALU.add, ALU.mult)
                sm["fdve"].inc(ins, 1)
                if fq >= 2:
                    st["h1tr"].wait_done(v, fq - 2)
                v.drain()
                ins = v.tensor_copy(h1lp[:, par * D:(par + 1) * D],
                                    h1f32[:, par * D:(par + 1) * D])
                sm["fin"].inc(ins, 1)

            @block.vector
            def _(v):
                for qb in range(NQB):
                    for kc in range(qb + 1):
                        gl = gl_of(qb, kc)
                        for qs in range(8):
                            if kc == 0 and qb >= 1:
                                dve_finalize(v, qb - 1, qs)
                            pr = pairs[gl * 8 + qs]
                            gp = pr["gp"]
                            sb = (gp % 2) * 12
                            first = pr["first"]
                            if pr["nh"] == 2:
                                v.wait_ge(sm["sch"].h, cum_sch[gp + 1])
                                v.reduce_max(sml[:, sb:sb + 1],
                                             s_ps[gp % 2][:, 0:512], axis=AX)
                                v.wait_ge(sm["sc"].h, gp + 1)
                                v.reduce_max(sml[:, sb + 1:sb + 2],
                                             s_ps[gp % 2][:, 512:1024], axis=AX)
                                v.drain()
                                v.tensor_tensor(sml[:, sb:sb + 1],
                                                sml[:, sb:sb + 1],
                                                sml[:, sb + 1:sb + 2], ALU.max)
                            else:
                                v.wait_ge(sm["sc"].h, gp + 1)
                                v.reduce_max(sml[:, sb:sb + 1],
                                             s_ps[gp % 2][:, 0:pr["W"]], axis=AX)
                            v.drain()
                            if first:
                                v.tensor_scalar_mul(sml[:, sb + 2:sb + 3],
                                                    sml[:, sb:sb + 1], SCALE)
                            else:
                                v.tensor_scalar(sml[:, sb + 2:sb + 3],
                                                sml[:, sb:sb + 1], SCALE,
                                                m_sb[:, qs:qs + 1],
                                                ALU.mult, ALU.max)
                            v.drain()
                            if not first:
                                v.tensor_tensor(sml[:, sb + 3:sb + 4],
                                                m_sb[:, qs:qs + 1],
                                                sml[:, sb + 2:sb + 3], ALU.subtract)
                            v.tensor_scalar_mul(sml[:, sb + 4:sb + 5],
                                                sml[:, sb + 2:sb + 3], -1.0)
                            v.drain()
                            ins = v.tensor_copy(m_sb[:, qs:qs + 1],
                                                sml[:, sb + 2:sb + 3])
                            sm["dve1"].inc(ins, 1)
                            if pr["diag"]:
                                v.wait_ge(sm["msk"].h, pr["di"] + 1)
                                v.reduce_sum(sml[:, sb + 5:sb + 6],
                                             p_sb[:, (gp % 3) * KC:(gp % 3) * KC + pr["W"]],
                                             axis=AX)
                                v.drain()
                            v.wait_ge(sm["av"].h, gp + 1)
                            if first:
                                v.tensor_copy(l_sb[:, qs:qs + 1],
                                              sml[:, sb + 5:sb + 6])
                                for dh in range(2):
                                    ins = v.tensor_copy(
                                        o_acc[:, qs * D + dh * 512:qs * D + (dh + 1) * 512],
                                        av_ps[gp % 2][:, dh * 512:(dh + 1) * 512])
                            else:
                                v.scalar_tensor_tensor(
                                    l_sb[:, qs:qs + 1], l_sb[:, qs:qs + 1],
                                    sml[:, sb + 6:sb + 7], sml[:, sb + 5:sb + 6],
                                    ALU.mult, ALU.add)
                                for dh in range(2):
                                    ins = v.scalar_tensor_tensor(
                                        o_acc[:, qs * D + dh * 512:qs * D + (dh + 1) * 512],
                                        o_acc[:, qs * D + dh * 512:qs * D + (dh + 1) * 512],
                                        sml[:, sb + 6:sb + 7],
                                        av_ps[gp % 2][:, dh * 512:(dh + 1) * 512],
                                        ALU.mult, ALU.add)
                            sm["oup"].inc(ins, 1)
                for qs in range(8):
                    dve_finalize(v, NQB - 1, qs)

            def act_finalize(sc, qb, qs):
                fq = qb * 8 + qs
                sc.wait_ge(sm["fdve"].h, 3 * fq + 1)
                ins = sc.activation(scr_a[:, :], h_pre[:, :], AF.Copy,
                                    accum_out=sml[:, 17:18])
                sm["fact"].inc(ins, 1)
                sc.wait_ge(sm["fdve"].h, 3 * fq + 2)
                sc.activation(sml[:, 20:21], sml[:, 19:20], AF.Ln)
                sc.drain()
                ins = sc.activation(sml[:, 20:21], sml[:, 20:21],
                                    AF.Exp, scale=-0.5)
                sm["fact"].inc(ins, 1)

            @block.scalar
            def _(sc):
                for qb in range(NQB):
                    for kc in range(qb + 1):
                        gl = gl_of(qb, kc)
                        for qs in range(8):
                            if kc == 0 and qb >= 1:
                                act_finalize(sc, qb - 1, qs)
                            pr = pairs[gl * 8 + qs]
                            gp = pr["gp"]
                            sb = (gp % 2) * 12
                            sc.wait_ge(sm["dve1"].h, gp + 1)
                            if not pr["first"]:
                                sc.activation(sml[:, sb + 6:sb + 7],
                                              sml[:, sb + 3:sb + 4], AF.Exp)
                                sc.drain()
                            if gp >= 3:
                                st["pt"].wait_done(sc, gp - 3)
                                sc.wait_ge(sm["oup"].h, gp - 2)
                            ins = sc.activation(
                                p_sb[:, (gp % 3) * KC:(gp % 3) * KC + pr["W"]],
                                s_ps[gp % 2][:, 0:pr["W"]], AF.Exp,
                                bias=sml[:, sb + 4:sb + 5], scale=SCALE,
                                accum_out=(None if pr["diag"]
                                           else sml[:, sb + 5:sb + 6]))
                            sm["expd"].inc(ins, 1)
                for qs in range(8):
                    act_finalize(sc, NQB - 1, qs)

        with ExitStack() as p4:
            e4 = p4.enter_context
            h1_sb = e4(nc.sbuf_tensor("h1_sb", [P, 2 * ND * TS_T], BF16))
            wg_sb = e4(nc.sbuf_tensor("wg_sb", [P, 2 * ND * FCB * P], BF16))
            wu_sb = e4(nc.sbuf_tensor("wu_sb", [P, 2 * ND * FCB * P], BF16))
            wd_sb = e4(nc.sbuf_tensor("wd_sb", [P, 2 * FCB * D], BF16))
            a_silu = e4(nc.sbuf_tensor("a_silu", [P, 2 * TS_T], BF16))
            a_sb = e4(nc.sbuf_tensor("a_sb", [P, 2 * TS_T], BF16))
            h1res = e4(nc.sbuf_tensor("h1res", [P, 2 * 3 * D], F32))
            s_pre = e4(nc.sbuf_tensor("s_pre", [P, 3 * D], F32))
            scr4 = e4(nc.sbuf_tensor("scr4", [P, D], F32))
            scr4a = e4(nc.sbuf_tensor("scr4a", [P, D], F32))
            out_sb = e4(nc.sbuf_tensor("out_sb", [P, 4 * D], F32))
            sml4 = e4(nc.sbuf_tensor("sml4", [P, 24], F32))
            g_ps = e4(nc.psum_tensor("g_ps", [P, TS_T], F32))
            u_ps = e4(nc.psum_tensor("u_ps", [P, TS_T], F32))
            y_ps = [e4(nc.psum_tensor(f"y_ps{i}", [P, 512], F32)) for i in range(6)]

            fq2_of_ts = []
            acc = 0
            for (t0v, T) in ts_list:
                fq2_of_ts.append(acc)
                acc += T // P

            def wg_slice(gfcb, dc, fcl):
                base = (gfcb % 2) * ND * FCB * P + dc * FCB * P
                return wg_sb[:, base + fcl * P:base + (fcl + 1) * P]

            def wu_slice(gfcb, dc, fcl):
                base = (gfcb % 2) * ND * FCB * P + dc * FCB * P
                return wu_sb[:, base + fcl * P:base + (fcl + 1) * P]

            def wd_slice(gfcb, fcl, dh):
                base = (gfcb % 2) * FCB * D + fcl * D
                return wd_sb[:, base + dh * 512:base + (dh + 1) * 512]

            def p4_stores(s, ts):
                t0v, T = ts_list[ts]
                for tc in range(T // P):
                    fq2 = fq2_of_ts[ts] + tc
                    s.wait_ge(sm["h2dve"].h, 3 * fq2 + 3)
                    ins = s.dma_start(
                        out_d[t0v + tc * P:t0v + (tc + 1) * P, :],
                        out_sb[:, (fq2 % 4) * D:(fq2 % 4 + 1) * D])
                    st["outst"].inc(ins, fq2)

            @block.sync
            def _(s):
                s.wait_ge(sm["wcast"].h, WCAST_ALL)
                st["h1st"].wait_all(s)
                for ts in range(NTS):
                    for fcb in range(NFCB):
                        if fcb == NFCB // 2 and ts >= 1:
                            p4_stores(s, ts - 1)
                        gfcb = ts * NFCB + fcb
                        if gfcb >= 2:
                            s.wait_ge(sm["gmm"].h, 4 * (gfcb - 1))
                        ins = s.dma_start(
                            wg_sb[:, (gfcb % 2) * ND * FCB * P:
                                  ((gfcb % 2) + 1) * ND * FCB * P]
                            .rearrange("p (c f) -> p c f", c=ND),
                            glp.rearrange("(c p) f -> p c f", p=P)[
                                :, :, fcb * FCB * P:(fcb + 1) * FCB * P])
                        st["wgld"].inc(ins, gfcb)
                        if gfcb >= 2:
                            s.wait_ge(sm["umm"].h, 4 * (gfcb - 1))
                        ins = s.dma_start(
                            wu_sb[:, (gfcb % 2) * ND * FCB * P:
                                  ((gfcb % 2) + 1) * ND * FCB * P]
                            .rearrange("p (c f) -> p c f", c=ND),
                            ulp.rearrange("(c p) f -> p c f", p=P)[
                                :, :, fcb * FCB * P:(fcb + 1) * FCB * P])
                        st["wuld"].inc(ins, gfcb)
                p4_stores(s, NTS - 1)
                st["outst"].wait_all(s)

            def h1_res_load(g, ts):
                t0v, T = ts_list[ts]
                if ts >= 2:
                    g.wait_ge(sm["umm"].h, NFC * (ts - 1))
                ins = g.dma_start(
                    h1_sb[:, (ts % 2) * ND * TS_T:(ts % 2) * ND * TS_T + ND * T]
                    .rearrange("p (c t) -> p c t", c=ND),
                    h1fm.rearrange("(c p) t -> p c t", p=P)[:, :, t0v:t0v + T])
                st["h1ld"].inc(ins, ts)
                if ts >= 2:
                    g.wait_ge(sm["h2ev"].h, ts - 1)
                ins = g.dma_start(
                    h1res[:, (ts % 2) * 3 * D:(ts % 2) * 3 * D + (T // P) * D]
                    .rearrange("p (tc d) -> p tc d", tc=T // P),
                    h1tok.rearrange("(tc p) d -> tc p d", p=P)[
                        t0v // P:t0v // P + T // P].rearrange("tc p d -> p tc d"))
                st["resld"].inc(ins, ts)

            @block.gpsimd
            def _(g):
                g.wait_ge(sm["wcast"].h, WCAST_ALL)
                st["tokst"].wait_all(g)
                st["h1st"].wait_all(g)
                h1_res_load(g, 0)
                for ts in range(NTS):
                    t0v, T = ts_list[ts]
                    if ts + 1 < NTS:
                        h1_res_load(g, ts + 1)
                    for fcb in range(NFCB):
                        gfcb = ts * NFCB + fcb
                        if gfcb >= 2:
                            g.wait_ge(sm["ydone"].h, 4 * (gfcb - 1))
                        ins = g.dma_start(
                            wd_sb[:, (gfcb % 2) * FCB * D:((gfcb % 2) + 1) * FCB * D]
                            .rearrange("p (c d) -> p c d", c=FCB),
                            dlp.rearrange("(c p) d -> p c d", p=P)[
                                :, fcb * FCB:(fcb + 1) * FCB, :])
                        st["wdld"].inc(ins, gfcb)

            @block.tensor
            def _(t):
                def h1sl(ts, dc, T):
                    base = (ts % 2) * ND * TS_T + dc * T
                    return h1_sb[:, base:base + T]

                def y_mms(ts, fc, T, start, stop):
                    apar = (ts * NFC + fc) % 2
                    gfcb = ts * NFCB + fc // FCB
                    for tc in range(T // P):
                        for dh in range(2):
                            ins = t.matmul(
                                y_ps[tc * 2 + dh][:, :],
                                a_sb[:, apar * TS_T + tc * P:apar * TS_T + (tc + 1) * P],
                                wd_slice(gfcb, fc % FCB, dh),
                                start=start, stop=stop)
                    return ins

                for ts in range(NTS):
                    t0v, T = ts_list[ts]
                    st["h1ld"].wait_done(t, ts)
                    for fc in range(NFC):
                        gfc = ts * NFC + fc
                        gfcb = ts * NFCB + fc // FCB
                        if fc % FCB == 0:
                            st["wgld"].wait_done(t, gfcb)
                            st["wuld"].wait_done(t, gfcb)
                        if gfc >= 1:
                            t.wait_ge(sm["silu"].h, gfc)
                        for dc in range(ND):
                            ins = t.matmul(g_ps[:, 0:T],
                                           wg_slice(gfcb, dc, fc % FCB),
                                           h1sl(ts, dc, T),
                                           start=(dc == 0), stop=(dc == ND - 1))
                        sm["gmm"].inc(ins, 1)
                        if gfc >= 1:
                            t.wait_ge(sm["mul"].h, gfc)
                        for dc in range(ND):
                            ins = t.matmul(u_ps[:, 0:T],
                                           wu_slice(gfcb, dc, fc % FCB),
                                           h1sl(ts, dc, T),
                                           start=(dc == 0), stop=(dc == ND - 1))
                        sm["umm"].inc(ins, 1)
                        if fc >= 1:
                            fp = fc - 1
                            if fp % FCB == 0:
                                st["wdld"].wait_done(t, ts * NFCB + fp // FCB)
                            t.wait_ge(sm["mul"].h, gfc)
                            if fc == 1 and ts >= 1:
                                t.wait_ge(sm["h2ev"].h, ts)
                            ins = y_mms(ts, fp, T, start=(fp == 0), stop=False)
                            sm["ydone"].inc(ins, 1)
                    t.wait_ge(sm["mul"].h, ts * NFC + NFC)
                    ins = y_mms(ts, NFC - 1, T, start=False, stop=True)
                    sm["ydone"].inc(ins, 1)

            def ln2_lsum(sc, ts, tc):
                fq2 = fq2_of_ts[ts] + tc
                q = (fq2 % 3) * 8
                sc.wait_ge(sm["h2dve"].h, 3 * fq2 + 1)
                ins = sc.activation(scr4a[:, :],
                                    s_pre[:, (fq2 % 3) * D:(fq2 % 3 + 1) * D],
                                    AF.Copy, accum_out=sml4[:, q + 1:q + 2])
                sm["h2act"].inc(ins, 1)

            def ln2_rstd(sc, ts, tc):
                fq2 = fq2_of_ts[ts] + tc
                q = (fq2 % 3) * 8
                sc.wait_ge(sm["h2dve"].h, 3 * fq2 + 2)
                sc.activation(sml4[:, q + 4:q + 5], sml4[:, q + 3:q + 4], AF.Ln)
                sc.drain()
                ins = sc.activation(sml4[:, q + 4:q + 5], sml4[:, q + 4:q + 5],
                                    AF.Exp, scale=-0.5)
                sm["h2act"].inc(ins, 1)

            @block.scalar
            def _(sc):
                for ts in range(NTS):
                    t0v, T = ts_list[ts]
                    hooks = {}
                    if ts >= 1:
                        pT = ts_list[ts - 1][1]
                        for tc in range(pT // P):
                            hooks[12 + 24 * tc] = (ln2_lsum, tc)
                            hooks[20 + 24 * tc] = (ln2_rstd, tc)
                    for fc in range(NFC):
                        gfc = ts * NFC + fc
                        par = gfc % 2
                        sc.wait_ge(sm["gmm"].h, gfc + 1)
                        if gfc >= 2:
                            sc.wait_ge(sm["mul"].h, gfc - 1)
                        ins = sc.activation(a_silu[:, par * TS_T:par * TS_T + T],
                                            g_ps[:, 0:T], AF.Silu)
                        sm["silu"].inc(ins, 1)
                        if fc in hooks:
                            fn, tc = hooks[fc]
                            fn(sc, ts - 1, tc)
                for tc in range(ts_list[NTS - 1][1] // P):
                    ln2_lsum(sc, NTS - 1, tc)
                    ln2_rstd(sc, NTS - 1, tc)

            def ln2_adds(v, ts, tc):
                fq2 = fq2_of_ts[ts] + tc
                sp = (fq2 % 3) * D
                rpar = ts % 2
                T = ts_list[ts][1]
                if tc == 0:
                    v.wait_ge(sm["ydone"].h, NFC * (ts + 1))
                    st["resld"].wait_done(v, ts)
                v.drain()
                for dh in range(2):
                    ins = v.tensor_tensor(
                        s_pre[:, sp + dh * 512:sp + (dh + 1) * 512],
                        y_ps[tc * 2 + dh][:, :],
                        h1res[:, rpar * 3 * D + tc * D + dh * 512:
                              rpar * 3 * D + tc * D + (dh + 1) * 512],
                        ALU.add)
                if tc == T // P - 1:
                    sm["h2ev"].inc(ins, 1)

            def ln2_ssq(v, ts, tc):
                fq2 = fq2_of_ts[ts] + tc
                sp = (fq2 % 3) * D
                q = (fq2 % 3) * 8
                v.drain()
                ins = v.scalar_tensor_tensor(
                    scr4[:, :], s_pre[:, sp:sp + D], 1.0, s_pre[:, sp:sp + D],
                    ALU.mult, ALU.mult, accum_out=sml4[:, q:q + 1])
                sm["h2dve"].inc(ins, 1)

            def ln2_meanvar(v, ts, tc):
                fq2 = fq2_of_ts[ts] + tc
                q = (fq2 % 3) * 8
                v.wait_ge(sm["h2act"].h, 2 * fq2 + 1)
                v.drain()
                v.tensor_scalar_mul(sml4[:, q + 2:q + 3], sml4[:, q + 1:q + 2],
                                    -1.0 / D)
                v.drain()
                v.tensor_tensor(sml4[:, q + 3:q + 4], sml4[:, q + 2:q + 3],
                                sml4[:, q + 2:q + 3], ALU.mult)
                v.drain()
                v.scalar_tensor_tensor(
                    sml4[:, q + 3:q + 4], sml4[:, q:q + 1], 1.0 / D,
                    sml4[:, q + 3:q + 4], ALU.mult, ALU.subtract)
                v.drain()
                ins = v.tensor_scalar_add(sml4[:, q + 3:q + 4],
                                          sml4[:, q + 3:q + 4], EPS)
                sm["h2dve"].inc(ins, 1)

            def ln2_apply(v, ts, tc):
                fq2 = fq2_of_ts[ts] + tc
                sp = (fq2 % 3) * D
                q = (fq2 % 3) * 8
                v.wait_ge(sm["h2act"].h, 2 * fq2 + 2)
                if fq2 >= 4:
                    st["outst"].wait_done(v, fq2 - 4)
                v.drain()
                ins = v.tensor_scalar(
                    out_sb[:, (fq2 % 4) * D:(fq2 % 4 + 1) * D],
                    s_pre[:, sp:sp + D], sml4[:, q + 2:q + 3],
                    sml4[:, q + 4:q + 5], ALU.add, ALU.mult)
                sm["h2dve"].inc(ins, 1)

            @block.vector
            def _(v):
                for ts in range(NTS):
                    t0v, T = ts_list[ts]
                    hooks = {}
                    if ts >= 1:
                        pT = ts_list[ts - 1][1]
                        for tc in range(pT // P):
                            ln2_adds(v, ts - 1, tc)
                            hooks[8 + 24 * tc] = (ln2_ssq, tc)
                            hooks[16 + 24 * tc] = (ln2_meanvar, tc)
                            hooks[24 + 24 * tc] = (ln2_apply, tc)
                    for fc in range(NFC):
                        gfc = ts * NFC + fc
                        par = gfc % 2
                        v.wait_ge(sm["silu"].h, gfc + 1)
                        v.wait_ge(sm["umm"].h, gfc + 1)
                        if gfc >= 2:
                            v.wait_ge(sm["ydone"].h, gfc - 1)
                        ins = v.tensor_tensor(a_sb[:, par * TS_T:par * TS_T + T],
                                              a_silu[:, par * TS_T:par * TS_T + T],
                                              u_ps[:, 0:T], ALU.mult)
                        sm["mul"].inc(ins, 1)
                        if fc in hooks:
                            fn, tc = hooks[fc]
                            fn(v, ts - 1, tc)
                for tc in range(ts_list[NTS - 1][1] // P):
                    ln2_adds(v, NTS - 1, tc)
                    ln2_ssq(v, NTS - 1, tc)
                    ln2_meanvar(v, NTS - 1, tc)
                    ln2_apply(v, NTS - 1, tc)

    return nc


def kernel(L0_attn_x, L0_attn_Qw, L0_attn_Kw, L0_attn_Vw,
           L0_ffn_Wg, L0_ffn_Wu, L0_ffn_Wd):
    x = np.ascontiguousarray(np.asarray(L0_attn_x, dtype=np.float32))
    qw = np.ascontiguousarray(np.asarray(L0_attn_Qw, dtype=np.float32))
    kw = np.ascontiguousarray(np.asarray(L0_attn_Kw, dtype=np.float32))
    vw = np.ascontiguousarray(np.asarray(L0_attn_Vw, dtype=np.float32))
    wg = np.ascontiguousarray(np.asarray(L0_ffn_Wg, dtype=np.float32))
    wu = np.ascontiguousarray(np.asarray(L0_ffn_Wu, dtype=np.float32))
    wd = np.ascontiguousarray(np.asarray(L0_ffn_Wd, dtype=np.float32))
    B, S, _ = x.shape
    nc = build(S)
    in_maps = [dict(x=x[b], qw=qw, kw=kw, vw=vw, wg=wg, wu=wu, wd=wd)
               for b in range(B)]
    res = run_bass_kernel_spmd(nc, in_maps, core_ids=list(range(B)))
    return np.stack([r["out"] for r in res.results], axis=0)


# version 07
# speedup vs baseline: 1.0085x; 1.0085x over previous
import numpy as np
from contextlib import ExitStack

import concourse.bass as bass
import concourse.mybir as mybir
from concourse.bass_utils import run_bass_kernel_spmd

F32 = mybir.dt.float32
BF16 = mybir.dt.bfloat16
AF = mybir.ActivationFunctionType
ALU = mybir.AluOpType
AX = mybir.AxisListType.X

P = 128
D = 1024
ND = D // P
FF = 32768
NFC = FF // P
FCB = 4
NFCB = NFC // FCB
TS_T = 384
QB = 1024
KC = 1024
G1 = 512
NTC = G1 // P
EPS = 1e-5
SCALE = 1.0 / 32.0


class Sem:
    def __init__(self, nc, name):
        self.h = nc.alloc_semaphore(name)
        self.n = 0

    def inc(self, ins, k=1):
        ins.then_inc(self.h, k)
        self.n += k


class DmaStream:

    def __init__(self, nc, name, nbufs=2):
        self.nb = nbufs
        self.s = [Sem(nc, name + str(i)) for i in range(nbufs)]
        self.cum = {}

    def inc(self, ins, i):
        s = self.s[i % self.nb]
        s.inc(ins, 16)
        self.cum[i] = s.n

    def wait_done(self, eng, i):
        eng.wait_ge(self.s[i % self.nb].h, self.cum[i])

    def wait_count(self, eng, i, per_index):
        eng.wait_ge(self.s[i % self.nb].h, 16 * per_index * (i // self.nb + 1))

    def wait_all(self, eng):
        for s in self.s:
            if s.n:
                eng.wait_ge(s.h, s.n)


def build(S=8192):
    nc = bass.Bass("TRN2", target_bir_lowering=False)

    x_d = nc.dram_tensor("x", [S, D], F32, kind="ExternalInput").ap()
    qw_d = nc.dram_tensor("qw", [D, D], F32, kind="ExternalInput").ap()
    kw_d = nc.dram_tensor("kw", [D, D], F32, kind="ExternalInput").ap()
    vw_d = nc.dram_tensor("vw", [D, D], F32, kind="ExternalInput").ap()
    wg_d = nc.dram_tensor("wg", [8, D, 4096], F32, kind="ExternalInput").ap()
    wu_d = nc.dram_tensor("wu", [8, D, 4096], F32, kind="ExternalInput").ap()
    wd_d = nc.dram_tensor("wd", [8, 4096, D], F32, kind="ExternalInput").ap()
    out_d = nc.dram_tensor("out", [S, D], F32, kind="ExternalOutput").ap()

    glp = nc.dram_tensor("glp", [D, FF], BF16, kind="Internal").ap()
    ulp = nc.dram_tensor("ulp", [D, FF], BF16, kind="Internal").ap()
    dlp = nc.dram_tensor("dlp", [FF, D], BF16, kind="Internal").ap()
    qfm = nc.dram_tensor("qfm", [D, S], BF16, kind="Internal").ap()
    kfm = nc.dram_tensor("kfm", [D, S], BF16, kind="Internal").ap()
    vtok = nc.dram_tensor("vtok", [S, D], BF16, kind="Internal").ap()
    h1tok = nc.dram_tensor("h1tok", [S, D], F32, kind="Internal").ap()
    h1fm = nc.dram_tensor("h1fm", [D, S], BF16, kind="Internal").ap()

    NG = S // G1
    NQB = S // QB
    ts_list = []
    t0 = 0
    while t0 < S:
        t = min(TS_T, S - t0)
        ts_list.append((t0, t))
        t0 += t
    NTS = len(ts_list)

    def gl_of(qb, kc):
        return qb * (qb + 1) // 2 + kc

    pairs = []
    ndiag = 0
    for qb in range(NQB):
        for kc in range(qb + 1):
            for qs in range(8):
                diag = kc == qb
                if diag:
                    nv = (qs + 1) * P
                    nh = (nv + 511) // 512
                    nksc = qs + 1
                    di = ndiag
                    ndiag += 1
                else:
                    nh, nksc, di = 2, 8, None
                pairs.append(dict(qb=qb, kc=kc, qs=qs, gl=gl_of(qb, kc),
                                  gp=len(pairs), diag=diag, nh=nh,
                                  nksc=nksc, W=nh * 512, di=di,
                                  first=(kc == 0)))
    NPAIR = len(pairs)
    cum_sch = [0]
    for pr in pairs:
        cum_sch.append(cum_sch[-1] + (1 if pr["nh"] == 2 else 0))

    with ExitStack() as top:
        sm = {k: Sem(nc, k) for k in [
            "wcast", "qkvw", "qmm", "kmm", "vmm", "qev", "kev", "vev",
            "sc", "dve1", "expd", "msk", "av", "oup", "fdve", "fact", "fin",
            "gmm", "umm", "silu", "mul", "ydone", "h2dve", "h2act", "h2ev",
            "xc", "sch",
        ]}
        st = {k: DmaStream(nc, k) for k in [
            "xlp", "xtr", "qst", "kst", "vst",
            "qld", "kld", "vld", "xres", "tokst", "h1tr", "h1st",
            "wgld", "wuld", "wdld", "h1ld", "resld",
        ]}
        st["pt"] = DmaStream(nc, "pt", nbufs=3)
        st["outst"] = DmaStream(nc, "outst", nbufs=4)
        block = top.enter_context(nc.Block())

        with ExitStack() as p1:
            e1 = p1.enter_context
            qw_sb = e1(nc.sbuf_tensor("qw_sb", [P, ND * D], BF16))
            kw_sb = e1(nc.sbuf_tensor("kw_sb", [P, ND * D], BF16))
            vw_sb = e1(nc.sbuf_tensor("vw_sb", [P, ND * D], BF16))
            x_lp = e1(nc.sbuf_tensor("x_lp", [P, 2 * NTC * D], BF16))
            x_f32 = e1(nc.sbuf_tensor("x_f32", [P, 2 * NTC * D], F32))
            x_fm = e1(nc.sbuf_tensor("x_fm", [P, 2 * ND * G1], BF16))
            q_stg = e1(nc.sbuf_tensor("q_stg", [P, 2 * 8 * 512], BF16))
            k_stg = e1(nc.sbuf_tensor("k_stg", [P, 2 * 8 * 512], BF16))
            v_stg = e1(nc.sbuf_tensor("v_stg", [P, 2 * 8 * 512], BF16))
            q_ps = [e1(nc.psum_tensor(f"q_ps{i}", [P, 512], F32)) for i in range(2)]
            k_ps = [e1(nc.psum_tensor(f"k_ps{i}", [P, 512], F32)) for i in range(2)]
            v_ps = [e1(nc.psum_tensor(f"v_ps{i}", [P, 512], F32)) for i in range(2)]

            def xfm(par, dc, lo, n):
                base = par * ND * G1 + dc * G1
                return x_fm[:, base + lo:base + lo + n]

            @block.gpsimd
            def _(g):
                for w_sb, w_dd in ((qw_sb, qw_d), (kw_sb, kw_d), (vw_sb, vw_d)):
                    ins = g.dma_start(
                        w_sb[:, :].rearrange("p (c o) -> p c o", c=ND),
                        w_dd.rearrange("(c p) o -> p c o", p=P))
                    sm["qkvw"].inc(ins, 16)

            QKVW_ALL = sm["qkvw"].n
            WCAST_ALL = 16 * 48

            def p1_stores(s, gi):
                par = gi % 2
                for stg, ev, store_st, dst in (
                        (q_stg, sm["qev"], st["qst"], qfm),
                        (k_stg, sm["kev"], st["kst"], kfm)):
                    s.wait_ge(ev.h, 8 * (gi + 1))
                    ins = s.dma_start(
                        dst.rearrange("(oc p) s -> p oc s", p=P)[
                            :, :, gi * G1:(gi + 1) * G1],
                        stg[:, par * 8 * 512:(par + 1) * 8 * 512]
                        .rearrange("p (oc s) -> p oc s", oc=ND))
                    store_st.inc(ins, gi)
                s.wait_ge(sm["vev"].h, 8 * (gi + 1))
                ins = s.dma_start(
                    vtok[gi * G1:(gi + 1) * G1, :]
                    .rearrange("(tc p) d -> p tc d", p=P),
                    v_stg[:, par * 8 * 512:(par + 1) * 8 * 512]
                    .rearrange("p (tc d) -> p tc d", tc=NTC))
                st["vst"].inc(ins, gi)

            def x_load(s, gi):
                par = gi % 2
                if gi >= 2:
                    s.wait_ge(sm["xc"].h, gi - 1)
                ins = s.dma_start(
                    x_f32[:, par * NTC * D:(par + 1) * NTC * D]
                    .rearrange("p (tc d) -> p tc d", tc=NTC),
                    x_d.rearrange("(tc p) d -> tc p d", p=P)[
                        gi * NTC:(gi + 1) * NTC].rearrange("tc p d -> p tc d"))
                st["xlp"].inc(ins, gi)

            @block.sync
            def _(s):
                x_load(s, 0)
                for gi in range(NG):
                    par = gi % 2
                    if gi + 1 < NG:
                        x_load(s, gi + 1)
                    s.wait_ge(sm["xc"].h, gi + 1)
                    if gi >= 2:
                        s.wait_ge(sm["vmm"].h, 8 * (gi - 1))
                    for tc in range(NTC):
                        dst = x_fm[:, par * ND * G1:(par + 1) * ND * G1] \
                            .rearrange("p (dc g) -> p dc g", dc=ND)[
                                :, :, tc * P:(tc + 1) * P]
                        srcv = x_lp[:, (par * NTC + tc) * D:
                                    (par * NTC + tc + 1) * D] \
                            .rearrange("p (dc c) -> p dc c", dc=ND)
                        ins = s.dma_start(dst, srcv, transpose=True)
                        st["xtr"].inc(ins, gi)
                    if gi >= 1:
                        p1_stores(s, gi - 1)
                p1_stores(s, NG - 1)

            @block.tensor
            def _(t):
                t.wait_ge(sm["qkvw"].h, QKVW_ALL)
                for gi in range(NG):
                    par = gi % 2
                    st["xtr"].wait_done(t, gi)
                    for w_sb, ps, mm, ev in ((qw_sb, q_ps, sm["qmm"], sm["qev"]),
                                             (kw_sb, k_ps, sm["kmm"], sm["kev"])):
                        for oc in range(ND):
                            j = gi * ND + oc
                            if j >= 2:
                                t.wait_ge(ev.h, j - 1)
                            for dc in range(ND):
                                ins = t.matmul(
                                    ps[j % 2][:, :],
                                    w_sb[:, dc * D + oc * P:dc * D + (oc + 1) * P],
                                    xfm(par, dc, 0, G1),
                                    start=(dc == 0), stop=(dc == ND - 1))
                            mm.inc(ins, 1)
                    for tc in range(NTC):
                        for dh in range(2):
                            j = gi * ND + tc * 2 + dh
                            if j >= 2:
                                t.wait_ge(sm["vev"].h, j - 1)
                            for dc in range(ND):
                                ins = t.matmul(
                                    v_ps[j % 2][:, :],
                                    xfm(par, dc, tc * P, P),
                                    vw_sb[:, dc * D + dh * 512:dc * D + (dh + 1) * 512],
                                    start=(dc == 0), stop=(dc == ND - 1))
                            sm["vmm"].inc(ins, 1)

            @block.scalar
            def _(sc):
                for gi in range(NG):
                    par = gi % 2
                    for ps, stg, mm, ev, store_st in (
                            (q_ps, q_stg, sm["qmm"], sm["qev"], st["qst"]),
                            (k_ps, k_stg, sm["kmm"], sm["kev"], st["kst"])):
                        if gi >= 2:
                            store_st.wait_done(sc, gi - 2)
                        for oc in range(ND):
                            j = gi * ND + oc
                            sc.wait_ge(mm.h, j + 1)
                            ins = sc.activation(
                                stg[:, (par * 8 + oc) * 512:(par * 8 + oc + 1) * 512],
                                ps[j % 2][:, :], AF.Copy)
                            ev.inc(ins, 1)

            @block.vector
            def _(v):
                for gi in range(NG):
                    par = gi % 2
                    st["xlp"].wait_done(v, gi)
                    if gi >= 2:
                        st["xtr"].wait_done(v, gi - 2)
                    ins = v.tensor_copy(
                        x_lp[:, par * NTC * D:(par + 1) * NTC * D],
                        x_f32[:, par * NTC * D:(par + 1) * NTC * D])
                    sm["xc"].inc(ins, 1)
                    if gi >= 2:
                        st["vst"].wait_done(v, gi - 2)
                    for tc in range(NTC):
                        for dh in range(2):
                            j = gi * ND + tc * 2 + dh
                            v.wait_ge(sm["vmm"].h, j + 1)
                            ins = v.tensor_copy(
                                v_stg[:, (par * 8 + tc * 2 + dh) * 512:
                                      (par * 8 + tc * 2 + dh + 1) * 512],
                                v_ps[j % 2][:, :])
                            sm["vev"].inc(ins, 1)

        with ExitStack() as p2:
            e2 = p2.enter_context
            q_sb = e2(nc.sbuf_tensor("q_sb", [P, 2 * ND * QB], BF16))
            k_sb = e2(nc.sbuf_tensor("k_sb", [P, 2 * ND * KC], BF16))
            v_sb = e2(nc.sbuf_tensor("v_sb", [P, 2 * (KC // P) * D], BF16))
            o_acc = e2(nc.sbuf_tensor("o_acc", [P, 8 * D], F32))
            p_sb = e2(nc.sbuf_tensor("p_sb", [P, 3 * KC], BF16))
            pt_sb = e2(nc.sbuf_tensor("pt_sb", [P, 3 * KC], BF16))
            m_sb = e2(nc.sbuf_tensor("m_sb", [P, 8], F32))
            l_sb = e2(nc.sbuf_tensor("l_sb", [P, 8], F32))
            sml = e2(nc.sbuf_tensor("sml", [P, 24], F32))
            x_res = e2(nc.sbuf_tensor("x_res", [P, 2 * D], F32))
            h_pre = e2(nc.sbuf_tensor("h_pre", [P, D], F32))
            scr = e2(nc.sbuf_tensor("scr", [P, D], F32))
            scr_a = e2(nc.sbuf_tensor("scr_a", [P, D], F32))
            h1f32 = e2(nc.sbuf_tensor("h1f32", [P, 2 * D], F32))
            h1lp = e2(nc.sbuf_tensor("h1lp", [P, 2 * D], BF16))
            h1fmt = e2(nc.sbuf_tensor("h1fmt", [P, 2 * D], BF16))
            s_ps = [e2(nc.psum_tensor(f"s_ps{i}", [P, KC], F32)) for i in range(2)]
            av_ps = [e2(nc.psum_tensor(f"av_ps{i}", [P, D], F32)) for i in range(2)]

            def qsl(qb, dc, qs):
                base = (qb % 2) * ND * QB + dc * QB
                return q_sb[:, base + qs * P:base + (qs + 1) * P]

            def ksl(gl, dc, lo, n):
                base = (gl % 2) * ND * KC + dc * KC
                return k_sb[:, base + lo:base + lo + n]

            def vsl(gl, ksc, lo, n):
                base = (gl % 2) * (KC // P) * D + ksc * D
                return v_sb[:, base + lo:base + lo + n]

            @block.sync
            def _(s):
                st["qst"].wait_all(s)
                st["kst"].wait_all(s)
                st["vst"].wait_all(s)
                def kv_load(qb, kc):
                    gl = gl_of(qb, kc)
                    if gl >= 2:
                        s.wait_ge(sm["sc"].h, 8 * (gl - 1))
                    ins = s.dma_start(
                        k_sb[:, (gl % 2) * ND * KC:((gl % 2) + 1) * ND * KC]
                        .rearrange("p (c k) -> p c k", c=ND),
                        kfm.rearrange("(c p) k -> p c k", p=P)[
                            :, :, kc * KC:(kc + 1) * KC])
                    st["kld"].inc(ins, gl)
                    if gl >= 2:
                        s.wait_ge(sm["av"].h, 8 * (gl - 1))
                    ins = s.dma_start(
                        v_sb[:, (gl % 2) * (KC // P) * D:((gl % 2) + 1) * (KC // P) * D]
                        .rearrange("p (k d) -> p k d", k=KC // P),
                        vtok.rearrange("(k p) d -> k p d", p=P)[
                            kc * (KC // P):(kc + 1) * (KC // P)]
                        .rearrange("k p d -> p k d"))
                    st["vld"].inc(ins, gl)

                def fin_dmas(s, qb, qs):
                    fq = qb * 8 + qs
                    par = fq % 2
                    row = qb * QB + qs * P
                    if fq >= 2:
                        s.wait_ge(sm["fdve"].h, 3 * (fq - 2) + 1)
                    ins = s.dma_start(x_res[:, par * D:(par + 1) * D],
                                      x_d[row:row + P, :])
                    st["xres"].inc(ins, fq)
                    s.wait_ge(sm["fdve"].h, 3 * fq + 3)
                    ins = s.dma_start(h1tok[row:row + P, :],
                                      h1f32[:, par * D:(par + 1) * D])
                    st["tokst"].inc(ins, fq)
                    s.wait_ge(sm["fin"].h, fq + 1)
                    if fq >= 2:
                        st["h1st"].wait_done(s, fq - 2)
                    ins = s.dma_start(
                        h1fmt[:, par * D:(par + 1) * D]
                        .rearrange("p (dc c) -> p dc c", dc=ND),
                        h1lp[:, par * D:(par + 1) * D]
                        .rearrange("p (dc c) -> p dc c", dc=ND),
                        transpose=True)
                    st["h1tr"].inc(ins, fq)
                    st["h1tr"].wait_done(s, fq)
                    ins = s.dma_start(
                        h1fm.rearrange("(c p) t -> p c t", p=P)[:, :, row:row + P],
                        h1fmt[:, par * D:(par + 1) * D]
                        .rearrange("p (c t) -> p c t", c=ND))
                    st["h1st"].inc(ins, fq)

                for qb in range(NQB):
                    if qb == 0:
                        ins = s.dma_start(
                            q_sb[:, 0:ND * QB].rearrange("p (c q) -> p c q", c=ND),
                            qfm.rearrange("(c p) q -> p c q", p=P)[:, :, 0:QB])
                        st["qld"].inc(ins, 0)
                        kv_load(0, 0)
                    for kc in range(qb + 1):
                        gl = gl_of(qb, kc)
                        if kc + 1 <= qb:
                            kv_load(qb, kc + 1)
                        for qs in range(8):
                            if kc == 0 and qb >= 1:
                                fin_dmas(s, qb - 1, qs)
                            pr = pairs[gl * 8 + qs]
                            gp = pr["gp"]
                            par3 = gp % 3
                            s.wait_ge(sm["expd"].h, gp + 1)
                            if pr["diag"]:
                                s.wait_ge(sm["msk"].h, pr["di"] + 1)
                            if gp >= 3:
                                s.wait_ge(sm["av"].h, gp - 2)
                            nk = pr["nksc"]
                            ins = s.dma_start(
                                pt_sb[:, par3 * KC:par3 * KC + nk * P]
                                .rearrange("p (k c) -> p k c", k=nk),
                                p_sb[:, par3 * KC:par3 * KC + nk * P]
                                .rearrange("p (k c) -> p k c", k=nk),
                                transpose=True)
                            st["pt"].inc(ins, gp)
                    if qb + 1 < NQB:
                        if qb >= 1:
                            s.wait_ge(sm["sc"].h, 8 * (gl_of(qb - 1, qb - 1) + 1))
                        ins = s.dma_start(
                            q_sb[:, ((qb + 1) % 2) * ND * QB:((qb + 1) % 2 + 1) * ND * QB]
                            .rearrange("p (c q) -> p c q", c=ND),
                            qfm.rearrange("(c p) q -> p c q", p=P)[
                                :, :, (qb + 1) * QB:(qb + 2) * QB])
                        st["qld"].inc(ins, qb + 1)
                        kv_load(qb + 1, 0)
                for qs in range(8):
                    fin_dmas(s, NQB - 1, qs)

            @block.tensor
            def _(t):
                def emit_av(pr):
                    gp = pr["gp"]
                    par = gp % 2
                    par3 = gp % 3
                    if pr["qs"] == 0:
                        st["vld"].wait_done(t, pr["gl"])
                    st["pt"].wait_done(t, gp)
                    if gp >= 2:
                        t.wait_ge(sm["oup"].h, gp - 1)
                    for dh in range(2):
                        for ksc in range(pr["nksc"]):
                            ins = t.matmul(
                                av_ps[par][:, dh * 512:(dh + 1) * 512],
                                pt_sb[:, par3 * KC + ksc * P:par3 * KC + (ksc + 1) * P],
                                vsl(pr["gl"], ksc, dh * 512, 512),
                                start=(ksc == 0), stop=(ksc == pr["nksc"] - 1))
                    sm["av"].inc(ins, 1)

                for pr in pairs:
                    gp = pr["gp"]
                    par = gp % 2
                    if pr["qs"] == 0:
                        st["kld"].wait_done(t, pr["gl"])
                        if pr["kc"] == 0:
                            st["qld"].wait_done(t, pr["qb"])
                    if gp >= 2:
                        t.wait_ge(sm["expd"].h, gp - 1)
                    for nh in range(pr["nh"]):
                        for dc in range(ND):
                            ins = t.matmul(
                                s_ps[par][:, nh * 512:(nh + 1) * 512],
                                qsl(pr["qb"], dc, pr["qs"]),
                                ksl(pr["gl"], dc, nh * 512, 512),
                                start=(dc == 0), stop=(dc == ND - 1))
                        if nh == 0 and pr["nh"] == 2:
                            sm["sch"].inc(ins, 1)
                    sm["sc"].inc(ins, 1)
                    if gp >= 2 and pairs[gp - 2]["qb"] == pr["qb"]:
                        emit_av(pairs[gp - 2])
                    if pr["kc"] == pr["qb"] and pr["qs"] == 7:
                        if gp >= 1 and pairs[gp - 1]["qb"] == pr["qb"]:
                            emit_av(pairs[gp - 1])
                        emit_av(pr)

            @block.gpsimd
            def _(g):
                chunks = []
                for e in range(8):
                    for h in range(2):
                        chunks.append((glp[:, e * 4096 + h * 2048:e * 4096 + (h + 1) * 2048],
                                       wg_d[e][:, h * 2048:(h + 1) * 2048]))
                        chunks.append((ulp[:, e * 4096 + h * 2048:e * 4096 + (h + 1) * 2048],
                                       wu_d[e][:, h * 2048:(h + 1) * 2048]))
                        chunks.append((dlp[e * 4096 + h * 2048:e * 4096 + (h + 1) * 2048, :],
                                       wd_d[e][h * 2048:(h + 1) * 2048, :]))
                NCH = len(chunks)
                assert 16 * NCH == WCAST_ALL
                n_p1 = 10
                items = []
                for i in range(NCH):
                    if i < n_p1:
                        items.append((-1000 + i, "cast", i))
                    else:
                        frac = (i - n_p1) / max(1, NCH - n_p1 - 1)
                        items.append((int((frac ** 1.3) * 0.8 * NPAIR), "cast", i))
                for pr in pairs:
                    if pr["diag"]:
                        items.append((pr["gp"] + 1, "mask", pr))
                items.sort(key=lambda x: (x[0], x[1] == "cast"))
                for key, kind, payload in items:
                    if kind == "cast":
                        i = payload
                        if i < n_p1:
                            g.wait_ge(sm["vmm"].h, min(8 * NG, 16 * i))
                        else:
                            g.wait_ge(sm["sc"].h, min(NPAIR, max(0, key)))
                        ins = g.dma_start(*chunks[i])
                        sm["wcast"].inc(ins, 16)
                    else:
                        pr = payload
                        gp, qs = pr["gp"], pr["qs"]
                        par3 = gp % 3
                        g.wait_ge(sm["expd"].h, gp + 1)
                        ins = g.affine_select(
                            p_sb[:, par3 * KC:par3 * KC + pr["W"]],
                            p_sb[:, par3 * KC:par3 * KC + pr["W"]],
                            [[-1, pr["W"]]], ALU.is_ge, 0.0,
                            base=qs * P, channel_multiplier=1)
                        sm["msk"].inc(ins, 1)

            def dve_finalize(v, qb, qs):
                fq = qb * 8 + qs
                par = fq % 2
                st["xres"].wait_done(v, fq)
                v.drain()
                v.reciprocal(sml[:, 22:23], l_sb[:, qs:qs + 1])
                v.drain()
                ins = v.scalar_tensor_tensor(
                    h_pre[:, :], o_acc[:, qs * D:(qs + 1) * D],
                    sml[:, 22:23], x_res[:, par * D:(par + 1) * D],
                    ALU.mult, ALU.add)
                sm["fdve"].inc(ins, 1)
                v.drain()
                v.scalar_tensor_tensor(
                    scr[:, :], h_pre[:, :], 1.0, h_pre[:, :],
                    ALU.mult, ALU.mult, accum_out=sml[:, 16:17])
                v.wait_ge(sm["fact"].h, 2 * fq + 1)
                v.tensor_scalar_mul(sml[:, 18:19], sml[:, 17:18], -1.0 / D)
                v.drain()
                v.tensor_tensor(sml[:, 19:20], sml[:, 18:19],
                                sml[:, 18:19], ALU.mult)
                v.drain()
                v.scalar_tensor_tensor(
                    sml[:, 19:20], sml[:, 16:17], 1.0 / D,
                    sml[:, 19:20], ALU.mult, ALU.subtract)
                v.drain()
                ins = v.tensor_scalar_add(sml[:, 19:20],
                                          sml[:, 19:20], EPS)
                sm["fdve"].inc(ins, 1)
                v.wait_ge(sm["fact"].h, 2 * fq + 2)
                if fq >= 2:
                    st["tokst"].wait_done(v, fq - 2)
                v.drain()
                ins = v.tensor_scalar(h1f32[:, par * D:(par + 1) * D],
                                      h_pre[:, :], sml[:, 18:19],
                                      sml[:, 20:21], ALU.add, ALU.mult)
                sm["fdve"].inc(ins, 1)
                if fq >= 2:
                    st["h1tr"].wait_done(v, fq - 2)
                v.drain()
                ins = v.tensor_copy(h1lp[:, par * D:(par + 1) * D],
                                    h1f32[:, par * D:(par + 1) * D])
                sm["fin"].inc(ins, 1)

            @block.vector
            def _(v):
                for qb in range(NQB):
                    for kc in range(qb + 1):
                        gl = gl_of(qb, kc)
                        for qs in range(8):
                            if kc == 0 and qb >= 1:
                                dve_finalize(v, qb - 1, qs)
                            pr = pairs[gl * 8 + qs]
                            gp = pr["gp"]
                            sb = (gp % 2) * 12
                            first = pr["first"]
                            if pr["nh"] == 2:
                                v.wait_ge(sm["sch"].h, cum_sch[gp + 1])
                                v.reduce_max(sml[:, sb:sb + 1],
                                             s_ps[gp % 2][:, 0:512], axis=AX)
                                v.wait_ge(sm["sc"].h, gp + 1)
                                v.reduce_max(sml[:, sb + 1:sb + 2],
                                             s_ps[gp % 2][:, 512:1024], axis=AX)
                                v.drain()
                                v.tensor_tensor(sml[:, sb:sb + 1],
                                                sml[:, sb:sb + 1],
                                                sml[:, sb + 1:sb + 2], ALU.max)
                            else:
                                v.wait_ge(sm["sc"].h, gp + 1)
                                v.reduce_max(sml[:, sb:sb + 1],
                                             s_ps[gp % 2][:, 0:pr["W"]], axis=AX)
                            v.drain()
                            if first:
                                v.tensor_scalar_mul(sml[:, sb + 2:sb + 3],
                                                    sml[:, sb:sb + 1], SCALE)
                            else:
                                v.tensor_scalar(sml[:, sb + 2:sb + 3],
                                                sml[:, sb:sb + 1], SCALE,
                                                m_sb[:, qs:qs + 1],
                                                ALU.mult, ALU.max)
                            v.drain()
                            if not first:
                                v.tensor_tensor(sml[:, sb + 3:sb + 4],
                                                m_sb[:, qs:qs + 1],
                                                sml[:, sb + 2:sb + 3], ALU.subtract)
                            v.tensor_scalar_mul(sml[:, sb + 4:sb + 5],
                                                sml[:, sb + 2:sb + 3], -1.0)
                            v.drain()
                            ins = v.tensor_copy(m_sb[:, qs:qs + 1],
                                                sml[:, sb + 2:sb + 3])
                            sm["dve1"].inc(ins, 1)
                            if pr["diag"]:
                                v.wait_ge(sm["msk"].h, pr["di"] + 1)
                                v.reduce_sum(sml[:, sb + 5:sb + 6],
                                             p_sb[:, (gp % 3) * KC:(gp % 3) * KC + pr["W"]],
                                             axis=AX)
                                v.drain()
                            v.wait_ge(sm["av"].h, gp + 1)
                            if first:
                                v.tensor_copy(l_sb[:, qs:qs + 1],
                                              sml[:, sb + 5:sb + 6])
                                for dh in range(2):
                                    ins = v.tensor_copy(
                                        o_acc[:, qs * D + dh * 512:qs * D + (dh + 1) * 512],
                                        av_ps[gp % 2][:, dh * 512:(dh + 1) * 512])
                            else:
                                v.scalar_tensor_tensor(
                                    l_sb[:, qs:qs + 1], l_sb[:, qs:qs + 1],
                                    sml[:, sb + 6:sb + 7], sml[:, sb + 5:sb + 6],
                                    ALU.mult, ALU.add)
                                for dh in range(2):
                                    ins = v.scalar_tensor_tensor(
                                        o_acc[:, qs * D + dh * 512:qs * D + (dh + 1) * 512],
                                        o_acc[:, qs * D + dh * 512:qs * D + (dh + 1) * 512],
                                        sml[:, sb + 6:sb + 7],
                                        av_ps[gp % 2][:, dh * 512:(dh + 1) * 512],
                                        ALU.mult, ALU.add)
                            sm["oup"].inc(ins, 1)
                for qs in range(8):
                    dve_finalize(v, NQB - 1, qs)

            def act_finalize(sc, qb, qs):
                fq = qb * 8 + qs
                sc.wait_ge(sm["fdve"].h, 3 * fq + 1)
                ins = sc.activation(scr_a[:, :], h_pre[:, :], AF.Copy,
                                    accum_out=sml[:, 17:18])
                sm["fact"].inc(ins, 1)
                sc.wait_ge(sm["fdve"].h, 3 * fq + 2)
                sc.activation(sml[:, 20:21], sml[:, 19:20], AF.Ln)
                sc.drain()
                ins = sc.activation(sml[:, 20:21], sml[:, 20:21],
                                    AF.Exp, scale=-0.5)
                sm["fact"].inc(ins, 1)

            @block.scalar
            def _(sc):
                for qb in range(NQB):
                    for kc in range(qb + 1):
                        gl = gl_of(qb, kc)
                        for qs in range(8):
                            if kc == 0 and qb >= 1:
                                act_finalize(sc, qb - 1, qs)
                            pr = pairs[gl * 8 + qs]
                            gp = pr["gp"]
                            sb = (gp % 2) * 12
                            sc.wait_ge(sm["dve1"].h, gp + 1)
                            if not pr["first"]:
                                sc.activation(sml[:, sb + 6:sb + 7],
                                              sml[:, sb + 3:sb + 4], AF.Exp)
                                sc.drain()
                            if gp >= 3:
                                st["pt"].wait_done(sc, gp - 3)
                                sc.wait_ge(sm["oup"].h, gp - 2)
                            ins = sc.activation(
                                p_sb[:, (gp % 3) * KC:(gp % 3) * KC + pr["W"]],
                                s_ps[gp % 2][:, 0:pr["W"]], AF.Exp,
                                bias=sml[:, sb + 4:sb + 5], scale=SCALE,
                                accum_out=(None if pr["diag"]
                                           else sml[:, sb + 5:sb + 6]))
                            sm["expd"].inc(ins, 1)
                for qs in range(8):
                    act_finalize(sc, NQB - 1, qs)

        with ExitStack() as p4:
            e4 = p4.enter_context
            h1_sb = e4(nc.sbuf_tensor("h1_sb", [P, 2 * ND * TS_T], BF16))
            wg_sb = e4(nc.sbuf_tensor("wg_sb", [P, 2 * ND * FCB * P], BF16))
            wu_sb = e4(nc.sbuf_tensor("wu_sb", [P, 2 * ND * FCB * P], BF16))
            wd_sb = e4(nc.sbuf_tensor("wd_sb", [P, 2 * FCB * D], BF16))
            a_silu = e4(nc.sbuf_tensor("a_silu", [P, 2 * TS_T], BF16))
            a_sb = e4(nc.sbuf_tensor("a_sb", [P, 2 * TS_T], BF16))
            h1res = e4(nc.sbuf_tensor("h1res", [P, 2 * 3 * D], F32))
            s_pre = e4(nc.sbuf_tensor("s_pre", [P, 3 * D], F32))
            scr4 = e4(nc.sbuf_tensor("scr4", [P, D], F32))
            scr4a = e4(nc.sbuf_tensor("scr4a", [P, D], F32))
            out_sb = e4(nc.sbuf_tensor("out_sb", [P, 4 * D], F32))
            sml4 = e4(nc.sbuf_tensor("sml4", [P, 24], F32))
            g_ps = e4(nc.psum_tensor("g_ps", [P, TS_T], F32))
            u_ps = e4(nc.psum_tensor("u_ps", [P, TS_T], F32))
            y_ps = [e4(nc.psum_tensor(f"y_ps{i}", [P, 512], F32)) for i in range(6)]

            fq2_of_ts = []
            acc = 0
            for (t0v, T) in ts_list:
                fq2_of_ts.append(acc)
                acc += T // P

            def wg_slice(gfcb, dc, fcl):
                base = (gfcb % 2) * ND * FCB * P + dc * FCB * P
                return wg_sb[:, base + fcl * P:base + (fcl + 1) * P]

            def wu_slice(gfcb, dc, fcl):
                base = (gfcb % 2) * ND * FCB * P + dc * FCB * P
                return wu_sb[:, base + fcl * P:base + (fcl + 1) * P]

            def wd_slice(gfcb, fcl, dh):
                base = (gfcb % 2) * FCB * D + fcl * D
                return wd_sb[:, base + dh * 512:base + (dh + 1) * 512]

            def p4_stores(s, ts):
                t0v, T = ts_list[ts]
                for tc in range(T // P):
                    fq2 = fq2_of_ts[ts] + tc
                    s.wait_ge(sm["h2dve"].h, 3 * fq2 + 3)
                    ins = s.dma_start(
                        out_d[t0v + tc * P:t0v + (tc + 1) * P, :],
                        out_sb[:, (fq2 % 4) * D:(fq2 % 4 + 1) * D])
                    st["outst"].inc(ins, fq2)

            @block.sync
            def _(s):
                s.wait_ge(sm["wcast"].h, WCAST_ALL)
                st["h1st"].wait_all(s)
                for ts in range(NTS):
                    for fcb in range(NFCB):
                        if fcb == NFCB // 2 and ts >= 1:
                            p4_stores(s, ts - 1)
                        gfcb = ts * NFCB + fcb
                        if gfcb >= 2:
                            s.wait_ge(sm["gmm"].h, 4 * (gfcb - 1))
                        ins = s.dma_start(
                            wg_sb[:, (gfcb % 2) * ND * FCB * P:
                                  ((gfcb % 2) + 1) * ND * FCB * P]
                            .rearrange("p (c f) -> p c f", c=ND),
                            glp.rearrange("(c p) f -> p c f", p=P)[
                                :, :, fcb * FCB * P:(fcb + 1) * FCB * P])
                        st["wgld"].inc(ins, gfcb)
                        if gfcb >= 2:
                            s.wait_ge(sm["umm"].h, 4 * (gfcb - 1))
                        ins = s.dma_start(
                            wu_sb[:, (gfcb % 2) * ND * FCB * P:
                                  ((gfcb % 2) + 1) * ND * FCB * P]
                            .rearrange("p (c f) -> p c f", c=ND),
                            ulp.rearrange("(c p) f -> p c f", p=P)[
                                :, :, fcb * FCB * P:(fcb + 1) * FCB * P])
                        st["wuld"].inc(ins, gfcb)
                p4_stores(s, NTS - 1)
                st["outst"].wait_all(s)

            def h1_res_load(g, ts):
                t0v, T = ts_list[ts]
                if ts >= 2:
                    g.wait_ge(sm["umm"].h, NFC * (ts - 1))
                ins = g.dma_start(
                    h1_sb[:, (ts % 2) * ND * TS_T:(ts % 2) * ND * TS_T + ND * T]
                    .rearrange("p (c t) -> p c t", c=ND),
                    h1fm.rearrange("(c p) t -> p c t", p=P)[:, :, t0v:t0v + T])
                st["h1ld"].inc(ins, ts)
                if ts >= 2:
                    g.wait_ge(sm["h2ev"].h, ts - 1)
                ins = g.dma_start(
                    h1res[:, (ts % 2) * 3 * D:(ts % 2) * 3 * D + (T // P) * D]
                    .rearrange("p (tc d) -> p tc d", tc=T // P),
                    h1tok.rearrange("(tc p) d -> tc p d", p=P)[
                        t0v // P:t0v // P + T // P].rearrange("tc p d -> p tc d"))
                st["resld"].inc(ins, ts)

            @block.gpsimd
            def _(g):
                g.wait_ge(sm["wcast"].h, WCAST_ALL)
                st["tokst"].wait_all(g)
                st["h1st"].wait_all(g)
                h1_res_load(g, 0)
                for ts in range(NTS):
                    t0v, T = ts_list[ts]
                    if ts + 1 < NTS:
                        h1_res_load(g, ts + 1)
                    for fcb in range(NFCB):
                        gfcb = ts * NFCB + fcb
                        if gfcb >= 2:
                            g.wait_ge(sm["ydone"].h, 4 * (gfcb - 1))
                        ins = g.dma_start(
                            wd_sb[:, (gfcb % 2) * FCB * D:((gfcb % 2) + 1) * FCB * D]
                            .rearrange("p (c d) -> p c d", c=FCB),
                            dlp.rearrange("(c p) d -> p c d", p=P)[
                                :, fcb * FCB:(fcb + 1) * FCB, :])
                        st["wdld"].inc(ins, gfcb)

            @block.tensor
            def _(t):
                def h1sl(ts, dc, T):
                    base = (ts % 2) * ND * TS_T + dc * T
                    return h1_sb[:, base:base + T]

                def y_mms(ts, fc, T, start, stop):
                    apar = (ts * NFC + fc) % 2
                    gfcb = ts * NFCB + fc // FCB
                    for tc in range(T // P):
                        for dh in range(2):
                            ins = t.matmul(
                                y_ps[tc * 2 + dh][:, :],
                                a_sb[:, apar * TS_T + tc * P:apar * TS_T + (tc + 1) * P],
                                wd_slice(gfcb, fc % FCB, dh),
                                start=start, stop=stop)
                    return ins

                for ts in range(NTS):
                    t0v, T = ts_list[ts]
                    st["h1ld"].wait_done(t, ts)
                    for fc in range(NFC):
                        gfc = ts * NFC + fc
                        gfcb = ts * NFCB + fc // FCB
                        if fc % FCB == 0:
                            st["wgld"].wait_done(t, gfcb)
                            st["wuld"].wait_done(t, gfcb)
                        if gfc >= 1:
                            t.wait_ge(sm["silu"].h, gfc)
                        for dc in range(ND):
                            ins = t.matmul(g_ps[:, 0:T],
                                           wg_slice(gfcb, dc, fc % FCB),
                                           h1sl(ts, dc, T),
                                           start=(dc == 0), stop=(dc == ND - 1))
                        sm["gmm"].inc(ins, 1)
                        if gfc >= 1:
                            t.wait_ge(sm["mul"].h, gfc)
                        for dc in range(ND):
                            ins = t.matmul(u_ps[:, 0:T],
                                           wu_slice(gfcb, dc, fc % FCB),
                                           h1sl(ts, dc, T),
                                           start=(dc == 0), stop=(dc == ND - 1))
                        sm["umm"].inc(ins, 1)
                        if fc >= 1:
                            fp = fc - 1
                            if fp % FCB == 0:
                                st["wdld"].wait_done(t, ts * NFCB + fp // FCB)
                            t.wait_ge(sm["mul"].h, gfc)
                            if fc == 1 and ts >= 1:
                                t.wait_ge(sm["h2ev"].h, ts)
                            ins = y_mms(ts, fp, T, start=(fp == 0), stop=False)
                            sm["ydone"].inc(ins, 1)
                    t.wait_ge(sm["mul"].h, ts * NFC + NFC)
                    ins = y_mms(ts, NFC - 1, T, start=False, stop=True)
                    sm["ydone"].inc(ins, 1)

            def ln2_lsum(sc, ts, tc):
                fq2 = fq2_of_ts[ts] + tc
                q = (fq2 % 3) * 8
                sc.wait_ge(sm["h2dve"].h, 3 * fq2 + 1)
                ins = sc.activation(scr4a[:, :],
                                    s_pre[:, (fq2 % 3) * D:(fq2 % 3 + 1) * D],
                                    AF.Copy, accum_out=sml4[:, q + 1:q + 2])
                sm["h2act"].inc(ins, 1)

            def ln2_rstd(sc, ts, tc):
                fq2 = fq2_of_ts[ts] + tc
                q = (fq2 % 3) * 8
                sc.wait_ge(sm["h2dve"].h, 3 * fq2 + 2)
                sc.activation(sml4[:, q + 4:q + 5], sml4[:, q + 3:q + 4], AF.Ln)
                sc.drain()
                ins = sc.activation(sml4[:, q + 4:q + 5], sml4[:, q + 4:q + 5],
                                    AF.Exp, scale=-0.5)
                sm["h2act"].inc(ins, 1)

            @block.scalar
            def _(sc):
                for ts in range(NTS):
                    t0v, T = ts_list[ts]
                    hooks = {}
                    if ts >= 1:
                        pT = ts_list[ts - 1][1]
                        for tc in range(pT // P):
                            hooks[14 + 40 * tc] = (ln2_lsum, tc)
                            hooks[30 + 40 * tc] = (ln2_rstd, tc)
                    for fc in range(NFC):
                        gfc = ts * NFC + fc
                        par = gfc % 2
                        sc.wait_ge(sm["gmm"].h, gfc + 1)
                        if gfc >= 2:
                            sc.wait_ge(sm["mul"].h, gfc - 1)
                        ins = sc.activation(a_silu[:, par * TS_T:par * TS_T + T],
                                            g_ps[:, 0:T], AF.Silu)
                        sm["silu"].inc(ins, 1)
                        if fc in hooks:
                            fn, tc = hooks[fc]
                            fn(sc, ts - 1, tc)
                for tc in range(ts_list[NTS - 1][1] // P):
                    ln2_lsum(sc, NTS - 1, tc)
                    ln2_rstd(sc, NTS - 1, tc)

            def ln2_adds(v, ts, tc):
                fq2 = fq2_of_ts[ts] + tc
                sp = (fq2 % 3) * D
                rpar = ts % 2
                T = ts_list[ts][1]
                if tc == 0:
                    v.wait_ge(sm["ydone"].h, NFC * (ts + 1))
                    st["resld"].wait_done(v, ts)
                v.drain()
                for dh in range(2):
                    ins = v.tensor_tensor(
                        s_pre[:, sp + dh * 512:sp + (dh + 1) * 512],
                        y_ps[tc * 2 + dh][:, :],
                        h1res[:, rpar * 3 * D + tc * D + dh * 512:
                              rpar * 3 * D + tc * D + (dh + 1) * 512],
                        ALU.add)
                if tc == T // P - 1:
                    sm["h2ev"].inc(ins, 1)

            def ln2_ssq(v, ts, tc):
                fq2 = fq2_of_ts[ts] + tc
                sp = (fq2 % 3) * D
                q = (fq2 % 3) * 8
                v.drain()
                ins = v.scalar_tensor_tensor(
                    scr4[:, :], s_pre[:, sp:sp + D], 1.0, s_pre[:, sp:sp + D],
                    ALU.mult, ALU.mult, accum_out=sml4[:, q:q + 1])
                sm["h2dve"].inc(ins, 1)

            def ln2_meanvar(v, ts, tc):
                fq2 = fq2_of_ts[ts] + tc
                q = (fq2 % 3) * 8
                v.wait_ge(sm["h2act"].h, 2 * fq2 + 1)
                v.drain()
                v.tensor_scalar_mul(sml4[:, q + 2:q + 3], sml4[:, q + 1:q + 2],
                                    -1.0 / D)
                v.drain()
                v.tensor_tensor(sml4[:, q + 3:q + 4], sml4[:, q + 2:q + 3],
                                sml4[:, q + 2:q + 3], ALU.mult)
                v.drain()
                v.scalar_tensor_tensor(
                    sml4[:, q + 3:q + 4], sml4[:, q:q + 1], 1.0 / D,
                    sml4[:, q + 3:q + 4], ALU.mult, ALU.subtract)
                v.drain()
                ins = v.tensor_scalar_add(sml4[:, q + 3:q + 4],
                                          sml4[:, q + 3:q + 4], EPS)
                sm["h2dve"].inc(ins, 1)

            def ln2_apply(v, ts, tc):
                fq2 = fq2_of_ts[ts] + tc
                sp = (fq2 % 3) * D
                q = (fq2 % 3) * 8
                v.wait_ge(sm["h2act"].h, 2 * fq2 + 2)
                if fq2 >= 4:
                    st["outst"].wait_done(v, fq2 - 4)
                v.drain()
                ins = v.tensor_scalar(
                    out_sb[:, (fq2 % 4) * D:(fq2 % 4 + 1) * D],
                    s_pre[:, sp:sp + D], sml4[:, q + 2:q + 3],
                    sml4[:, q + 4:q + 5], ALU.add, ALU.mult)
                sm["h2dve"].inc(ins, 1)

            @block.vector
            def _(v):
                for ts in range(NTS):
                    t0v, T = ts_list[ts]
                    hooks = {}
                    if ts >= 1:
                        pT = ts_list[ts - 1][1]
                        for tc in range(pT // P):
                            ln2_adds(v, ts - 1, tc)
                            hooks[8 + 40 * tc] = (ln2_ssq, tc)
                            hooks[24 + 40 * tc] = (ln2_meanvar, tc)
                            hooks[40 + 40 * tc] = (ln2_apply, tc)
                    for fc in range(NFC):
                        gfc = ts * NFC + fc
                        par = gfc % 2
                        v.wait_ge(sm["silu"].h, gfc + 1)
                        v.wait_ge(sm["umm"].h, gfc + 1)
                        if gfc >= 2:
                            v.wait_ge(sm["ydone"].h, gfc - 1)
                        ins = v.tensor_tensor(a_sb[:, par * TS_T:par * TS_T + T],
                                              a_silu[:, par * TS_T:par * TS_T + T],
                                              u_ps[:, 0:T], ALU.mult)
                        sm["mul"].inc(ins, 1)
                        if fc in hooks:
                            fn, tc = hooks[fc]
                            fn(v, ts - 1, tc)
                for tc in range(ts_list[NTS - 1][1] // P):
                    ln2_adds(v, NTS - 1, tc)
                    ln2_ssq(v, NTS - 1, tc)
                    ln2_meanvar(v, NTS - 1, tc)
                    ln2_apply(v, NTS - 1, tc)

    return nc


def kernel(L0_attn_x, L0_attn_Qw, L0_attn_Kw, L0_attn_Vw,
           L0_ffn_Wg, L0_ffn_Wu, L0_ffn_Wd):
    x = np.ascontiguousarray(np.asarray(L0_attn_x, dtype=np.float32))
    qw = np.ascontiguousarray(np.asarray(L0_attn_Qw, dtype=np.float32))
    kw = np.ascontiguousarray(np.asarray(L0_attn_Kw, dtype=np.float32))
    vw = np.ascontiguousarray(np.asarray(L0_attn_Vw, dtype=np.float32))
    wg = np.ascontiguousarray(np.asarray(L0_ffn_Wg, dtype=np.float32))
    wu = np.ascontiguousarray(np.asarray(L0_ffn_Wu, dtype=np.float32))
    wd = np.ascontiguousarray(np.asarray(L0_ffn_Wd, dtype=np.float32))
    B, S, _ = x.shape
    nc = build(S)
    in_maps = [dict(x=x[b], qw=qw, kw=kw, vw=vw, wg=wg, wu=wu, wd=wd)
               for b in range(B)]
    res = run_bass_kernel_spmd(nc, in_maps, core_ids=list(range(B)))
    return np.stack([r["out"] for r in res.results], axis=0)


# version 08
# speedup vs baseline: 1.0802x; 1.0711x over previous
import numpy as np
from contextlib import ExitStack

import concourse.bass as bass
import concourse.mybir as mybir
from concourse.bass_utils import run_bass_kernel_spmd

F32 = mybir.dt.float32
BF16 = mybir.dt.bfloat16
AF = mybir.ActivationFunctionType
ALU = mybir.AluOpType
AX = mybir.AxisListType.X

P = 128
D = 1024
ND = D // P
FF = 32768
NFC = FF // P
FCB = 4
NFCB = NFC // FCB
TS_T = 384
QB = 1024
KC = 1024
G1 = 512
NTC = G1 // P
EPS = 1e-5
SCALE = 1.0 / 32.0


class Sem:
    def __init__(self, nc, name):
        self.h = nc.alloc_semaphore(name)
        self.n = 0

    def inc(self, ins, k=1):
        ins.then_inc(self.h, k)
        self.n += k


class DmaStream:

    def __init__(self, nc, name, nbufs=2):
        self.nb = nbufs
        self.s = [Sem(nc, name + str(i)) for i in range(nbufs)]
        self.cum = {}

    def inc(self, ins, i):
        s = self.s[i % self.nb]
        s.inc(ins, 16)
        self.cum[i] = s.n

    def wait_done(self, eng, i):
        eng.wait_ge(self.s[i % self.nb].h, self.cum[i])

    def wait_count(self, eng, i, per_index):
        eng.wait_ge(self.s[i % self.nb].h, 16 * per_index * (i // self.nb + 1))

    def wait_all(self, eng):
        for s in self.s:
            if s.n:
                eng.wait_ge(s.h, s.n)


def build(S=8192):
    nc = bass.Bass("TRN2", target_bir_lowering=False)

    x_d = nc.dram_tensor("x", [S, D], F32, kind="ExternalInput").ap()
    qw_d = nc.dram_tensor("qw", [D, D], F32, kind="ExternalInput").ap()
    kw_d = nc.dram_tensor("kw", [D, D], F32, kind="ExternalInput").ap()
    vw_d = nc.dram_tensor("vw", [D, D], F32, kind="ExternalInput").ap()
    wg_d = nc.dram_tensor("wg", [8, D, 4096], F32, kind="ExternalInput").ap()
    wu_d = nc.dram_tensor("wu", [8, D, 4096], F32, kind="ExternalInput").ap()
    wd_d = nc.dram_tensor("wd", [8, 4096, D], F32, kind="ExternalInput").ap()
    out_d = nc.dram_tensor("out", [S, D], F32, kind="ExternalOutput").ap()

    glp = nc.dram_tensor("glp", [D, FF], BF16, kind="Internal").ap()
    ulp = nc.dram_tensor("ulp", [D, FF], BF16, kind="Internal").ap()
    dlp = nc.dram_tensor("dlp", [FF, D], BF16, kind="Internal").ap()
    qfm = nc.dram_tensor("qfm", [D, S], BF16, kind="Internal").ap()
    kfm = nc.dram_tensor("kfm", [D, S], BF16, kind="Internal").ap()
    vtok = nc.dram_tensor("vtok", [S, D], BF16, kind="Internal").ap()
    h1tok = nc.dram_tensor("h1tok", [S, D], F32, kind="Internal").ap()
    h1fm = nc.dram_tensor("h1fm", [D, S], BF16, kind="Internal").ap()

    NG = S // G1
    NQB = S // QB
    ts_list = []
    t0 = 0
    while t0 < S:
        t = min(TS_T, S - t0)
        ts_list.append((t0, t))
        t0 += t
    NTS = len(ts_list)

    def gl_of(qb, kc):
        return qb * (qb + 1) // 2 + kc

    pairs = []
    ndiag = 0
    for qb in range(NQB):
        for kc in range(qb + 1):
            for qs in range(8):
                diag = kc == qb
                if diag:
                    nv = (qs + 1) * P
                    nh = (nv + 511) // 512
                    nksc = qs + 1
                    di = ndiag
                    ndiag += 1
                else:
                    nh, nksc, di = 2, 8, None
                pairs.append(dict(qb=qb, kc=kc, qs=qs, gl=gl_of(qb, kc),
                                  gp=len(pairs), diag=diag, nh=nh,
                                  nksc=nksc, W=nh * 512, di=di,
                                  first=(kc == 0)))
    NPAIR = len(pairs)
    cum_sch = [0]
    for pr in pairs:
        cum_sch.append(cum_sch[-1] + (1 if pr["nh"] == 2 else 0))

    with ExitStack() as top:
        sm = {k: Sem(nc, k) for k in [
            "wcast", "qkvw", "qmm", "kmm", "vmm", "qev", "kev", "vev",
            "sc", "dve1", "expd", "msk", "av", "oup", "fdve", "fact", "fin",
            "gmm", "umm", "silu", "mul", "ydone", "h2dve", "h2act", "h2ev",
            "xc", "sch",
        ]}
        st = {k: DmaStream(nc, k) for k in [
            "xlp", "xtr", "qst", "kst", "vst",
            "qld", "kld", "vld", "xres", "tokst", "h1tr", "h1st",
            "wgld", "wuld", "wdld", "h1ld", "resld",
        ]}
        st["pt"] = DmaStream(nc, "pt", nbufs=3)
        st["outst"] = DmaStream(nc, "outst", nbufs=4)
        block = top.enter_context(nc.Block())

        with ExitStack() as p1:
            e1 = p1.enter_context
            qw_sb = e1(nc.sbuf_tensor("qw_sb", [P, ND * D], BF16))
            kw_sb = e1(nc.sbuf_tensor("kw_sb", [P, ND * D], BF16))
            vw_sb = e1(nc.sbuf_tensor("vw_sb", [P, ND * D], BF16))
            x_lp = e1(nc.sbuf_tensor("x_lp", [P, 2 * NTC * D], BF16))
            x_f32 = e1(nc.sbuf_tensor("x_f32", [P, 2 * NTC * D], F32))
            x_fm = e1(nc.sbuf_tensor("x_fm", [P, 2 * ND * G1], BF16))
            q_stg = e1(nc.sbuf_tensor("q_stg", [P, 2 * 8 * 512], BF16))
            k_stg = e1(nc.sbuf_tensor("k_stg", [P, 2 * 8 * 512], BF16))
            v_stg = e1(nc.sbuf_tensor("v_stg", [P, 2 * 8 * 512], BF16))
            q_ps = [e1(nc.psum_tensor(f"q_ps{i}", [P, 512], F32)) for i in range(2)]
            k_ps = [e1(nc.psum_tensor(f"k_ps{i}", [P, 512], F32)) for i in range(2)]
            v_ps = [e1(nc.psum_tensor(f"v_ps{i}", [P, 512], F32)) for i in range(2)]

            def xfm(par, dc, lo, n):
                base = par * ND * G1 + dc * G1
                return x_fm[:, base + lo:base + lo + n]

            @block.gpsimd
            def _(g):
                for w_sb, w_dd in ((qw_sb, qw_d), (kw_sb, kw_d), (vw_sb, vw_d)):
                    ins = g.dma_start(
                        w_sb[:, :].rearrange("p (c o) -> p c o", c=ND),
                        w_dd.rearrange("(c p) o -> p c o", p=P))
                    sm["qkvw"].inc(ins, 16)

            QKVW_ALL = sm["qkvw"].n
            WCAST_ALL = 16 * 48

            def p1_stores(s, gi):
                par = gi % 2
                for stg, ev, store_st, dst in (
                        (q_stg, sm["qev"], st["qst"], qfm),
                        (k_stg, sm["kev"], st["kst"], kfm)):
                    s.wait_ge(ev.h, 8 * (gi + 1))
                    ins = s.dma_start(
                        dst.rearrange("(oc p) s -> p oc s", p=P)[
                            :, :, gi * G1:(gi + 1) * G1],
                        stg[:, par * 8 * 512:(par + 1) * 8 * 512]
                        .rearrange("p (oc s) -> p oc s", oc=ND))
                    store_st.inc(ins, gi)
                s.wait_ge(sm["vev"].h, 8 * (gi + 1))
                ins = s.dma_start(
                    vtok[gi * G1:(gi + 1) * G1, :]
                    .rearrange("(tc p) d -> p tc d", p=P),
                    v_stg[:, par * 8 * 512:(par + 1) * 8 * 512]
                    .rearrange("p (tc d) -> p tc d", tc=NTC))
                st["vst"].inc(ins, gi)

            def x_load(s, gi):
                par = gi % 2
                if gi >= 2:
                    s.wait_ge(sm["xc"].h, gi - 1)
                ins = s.dma_start(
                    x_f32[:, par * NTC * D:(par + 1) * NTC * D]
                    .rearrange("p (tc d) -> p tc d", tc=NTC),
                    x_d.rearrange("(tc p) d -> tc p d", p=P)[
                        gi * NTC:(gi + 1) * NTC].rearrange("tc p d -> p tc d"))
                st["xlp"].inc(ins, gi)

            @block.sync
            def _(s):
                x_load(s, 0)
                for gi in range(NG):
                    par = gi % 2
                    if gi + 1 < NG:
                        x_load(s, gi + 1)
                    s.wait_ge(sm["xc"].h, gi + 1)
                    if gi >= 2:
                        s.wait_ge(sm["vmm"].h, 8 * (gi - 1))
                    for tc in range(NTC):
                        dst = x_fm[:, par * ND * G1:(par + 1) * ND * G1] \
                            .rearrange("p (dc g) -> p dc g", dc=ND)[
                                :, :, tc * P:(tc + 1) * P]
                        srcv = x_lp[:, (par * NTC + tc) * D:
                                    (par * NTC + tc + 1) * D] \
                            .rearrange("p (dc c) -> p dc c", dc=ND)
                        ins = s.dma_start(dst, srcv, transpose=True)
                        st["xtr"].inc(ins, gi)
                    if gi >= 1:
                        p1_stores(s, gi - 1)
                p1_stores(s, NG - 1)

            @block.tensor
            def _(t):
                t.wait_ge(sm["qkvw"].h, QKVW_ALL)
                for gi in range(NG):
                    par = gi % 2
                    st["xtr"].wait_done(t, gi)
                    for w_sb, ps, mm, ev in ((qw_sb, q_ps, sm["qmm"], sm["qev"]),
                                             (kw_sb, k_ps, sm["kmm"], sm["kev"])):
                        for oc in range(ND):
                            j = gi * ND + oc
                            if j >= 2:
                                t.wait_ge(ev.h, j - 1)
                            for dc in range(ND):
                                ins = t.matmul(
                                    ps[j % 2][:, :],
                                    w_sb[:, dc * D + oc * P:dc * D + (oc + 1) * P],
                                    xfm(par, dc, 0, G1),
                                    start=(dc == 0), stop=(dc == ND - 1))
                            mm.inc(ins, 1)
                    for tc in range(NTC):
                        for dh in range(2):
                            j = gi * ND + tc * 2 + dh
                            if j >= 2:
                                t.wait_ge(sm["vev"].h, j - 1)
                            for dc in range(ND):
                                ins = t.matmul(
                                    v_ps[j % 2][:, :],
                                    xfm(par, dc, tc * P, P),
                                    vw_sb[:, dc * D + dh * 512:dc * D + (dh + 1) * 512],
                                    start=(dc == 0), stop=(dc == ND - 1))
                            sm["vmm"].inc(ins, 1)

            @block.scalar
            def _(sc):
                for gi in range(NG):
                    par = gi % 2
                    for ps, stg, mm, ev, store_st in (
                            (q_ps, q_stg, sm["qmm"], sm["qev"], st["qst"]),
                            (k_ps, k_stg, sm["kmm"], sm["kev"], st["kst"])):
                        if gi >= 2:
                            store_st.wait_done(sc, gi - 2)
                        for oc in range(ND):
                            j = gi * ND + oc
                            sc.wait_ge(mm.h, j + 1)
                            ins = sc.activation(
                                stg[:, (par * 8 + oc) * 512:(par * 8 + oc + 1) * 512],
                                ps[j % 2][:, :], AF.Copy)
                            ev.inc(ins, 1)

            @block.vector
            def _(v):
                for gi in range(NG):
                    par = gi % 2
                    st["xlp"].wait_done(v, gi)
                    if gi >= 2:
                        st["xtr"].wait_done(v, gi - 2)
                    ins = v.tensor_copy(
                        x_lp[:, par * NTC * D:(par + 1) * NTC * D],
                        x_f32[:, par * NTC * D:(par + 1) * NTC * D])
                    sm["xc"].inc(ins, 1)
                    if gi >= 2:
                        st["vst"].wait_done(v, gi - 2)
                    for tc in range(NTC):
                        for dh in range(2):
                            j = gi * ND + tc * 2 + dh
                            v.wait_ge(sm["vmm"].h, j + 1)
                            ins = v.tensor_copy(
                                v_stg[:, (par * 8 + tc * 2 + dh) * 512:
                                      (par * 8 + tc * 2 + dh + 1) * 512],
                                v_ps[j % 2][:, :])
                            sm["vev"].inc(ins, 1)

        with ExitStack() as p2:
            e2 = p2.enter_context
            q_sb = e2(nc.sbuf_tensor("q_sb", [P, 2 * ND * QB], BF16))
            k_sb = e2(nc.sbuf_tensor("k_sb", [P, 2 * ND * KC], BF16))
            v_sb = e2(nc.sbuf_tensor("v_sb", [P, 2 * (KC // P) * D], BF16))
            o_acc = e2(nc.sbuf_tensor("o_acc", [P, 8 * D], F32))
            p_sb = e2(nc.sbuf_tensor("p_sb", [P, 3 * KC], BF16))
            pt_sb = e2(nc.sbuf_tensor("pt_sb", [P, 3 * KC], BF16))
            m_sb = e2(nc.sbuf_tensor("m_sb", [P, 8], F32))
            l_sb = e2(nc.sbuf_tensor("l_sb", [P, 8], F32))
            sml = e2(nc.sbuf_tensor("sml", [P, 40], F32))
            x_res = e2(nc.sbuf_tensor("x_res", [P, 2 * D], F32))
            h_pre = e2(nc.sbuf_tensor("h_pre", [P, D], F32))
            scr = e2(nc.sbuf_tensor("scr", [P, D], F32))
            scr_a = e2(nc.sbuf_tensor("scr_a", [P, D], F32))
            h1f32 = e2(nc.sbuf_tensor("h1f32", [P, 2 * D], F32))
            h1lp = e2(nc.sbuf_tensor("h1lp", [P, 2 * D], BF16))
            h1fmt = e2(nc.sbuf_tensor("h1fmt", [P, 2 * D], BF16))
            s_ps = [e2(nc.psum_tensor(f"s_ps{i}", [P, KC], F32)) for i in range(2)]
            av_ps = [e2(nc.psum_tensor(f"av_ps{i}", [P, D], F32)) for i in range(2)]

            def qsl(qb, dc, qs):
                base = (qb % 2) * ND * QB + dc * QB
                return q_sb[:, base + qs * P:base + (qs + 1) * P]

            def ksl(gl, dc, lo, n):
                base = (gl % 2) * ND * KC + dc * KC
                return k_sb[:, base + lo:base + lo + n]

            def vsl(gl, ksc, lo, n):
                base = (gl % 2) * (KC // P) * D + ksc * D
                return v_sb[:, base + lo:base + lo + n]

            @block.sync
            def _(s):
                st["qst"].wait_all(s)
                st["kst"].wait_all(s)
                st["vst"].wait_all(s)
                def kv_load(qb, kc):
                    gl = gl_of(qb, kc)
                    if gl >= 2:
                        s.wait_ge(sm["sc"].h, 8 * (gl - 1))
                    ins = s.dma_start(
                        k_sb[:, (gl % 2) * ND * KC:((gl % 2) + 1) * ND * KC]
                        .rearrange("p (c k) -> p c k", c=ND),
                        kfm.rearrange("(c p) k -> p c k", p=P)[
                            :, :, kc * KC:(kc + 1) * KC])
                    st["kld"].inc(ins, gl)
                    if gl >= 2:
                        s.wait_ge(sm["av"].h, 8 * (gl - 1))
                    ins = s.dma_start(
                        v_sb[:, (gl % 2) * (KC // P) * D:((gl % 2) + 1) * (KC // P) * D]
                        .rearrange("p (k d) -> p k d", k=KC // P),
                        vtok.rearrange("(k p) d -> k p d", p=P)[
                            kc * (KC // P):(kc + 1) * (KC // P)]
                        .rearrange("k p d -> p k d"))
                    st["vld"].inc(ins, gl)

                def fin_dmas(s, qb, qs):
                    fq = qb * 8 + qs
                    par = fq % 2
                    row = qb * QB + qs * P
                    if fq >= 2:
                        s.wait_ge(sm["fdve"].h, 3 * (fq - 2) + 1)
                    ins = s.dma_start(x_res[:, par * D:(par + 1) * D],
                                      x_d[row:row + P, :])
                    st["xres"].inc(ins, fq)
                    s.wait_ge(sm["fdve"].h, 3 * fq + 3)
                    ins = s.dma_start(h1tok[row:row + P, :],
                                      h1f32[:, par * D:(par + 1) * D])
                    st["tokst"].inc(ins, fq)
                    s.wait_ge(sm["fin"].h, fq + 1)
                    if fq >= 2:
                        st["h1st"].wait_done(s, fq - 2)
                    ins = s.dma_start(
                        h1fmt[:, par * D:(par + 1) * D]
                        .rearrange("p (dc c) -> p dc c", dc=ND),
                        h1lp[:, par * D:(par + 1) * D]
                        .rearrange("p (dc c) -> p dc c", dc=ND),
                        transpose=True)
                    st["h1tr"].inc(ins, fq)
                    st["h1tr"].wait_done(s, fq)
                    ins = s.dma_start(
                        h1fm.rearrange("(c p) t -> p c t", p=P)[:, :, row:row + P],
                        h1fmt[:, par * D:(par + 1) * D]
                        .rearrange("p (c t) -> p c t", c=ND))
                    st["h1st"].inc(ins, fq)

                for qb in range(NQB):
                    if qb == 0:
                        ins = s.dma_start(
                            q_sb[:, 0:ND * QB].rearrange("p (c q) -> p c q", c=ND),
                            qfm.rearrange("(c p) q -> p c q", p=P)[:, :, 0:QB])
                        st["qld"].inc(ins, 0)
                        kv_load(0, 0)
                    for kc in range(qb + 1):
                        gl = gl_of(qb, kc)
                        if kc + 1 <= qb:
                            kv_load(qb, kc + 1)
                        for qs in range(8):
                            if kc == 0 and qb >= 1:
                                fin_dmas(s, qb - 1, qs)
                            pr = pairs[gl * 8 + qs]
                            gp = pr["gp"]
                            par3 = gp % 3
                            s.wait_ge(sm["expd"].h, gp + 1)
                            if pr["diag"]:
                                s.wait_ge(sm["msk"].h, pr["di"] + 1)
                            if gp >= 3:
                                s.wait_ge(sm["av"].h, gp - 2)
                            nk = pr["nksc"]
                            ins = s.dma_start(
                                pt_sb[:, par3 * KC:par3 * KC + nk * P]
                                .rearrange("p (k c) -> p k c", k=nk),
                                p_sb[:, par3 * KC:par3 * KC + nk * P]
                                .rearrange("p (k c) -> p k c", k=nk),
                                transpose=True)
                            st["pt"].inc(ins, gp)
                    if qb + 1 < NQB:
                        if qb >= 1:
                            s.wait_ge(sm["sc"].h, 8 * (gl_of(qb - 1, qb - 1) + 1))
                        ins = s.dma_start(
                            q_sb[:, ((qb + 1) % 2) * ND * QB:((qb + 1) % 2 + 1) * ND * QB]
                            .rearrange("p (c q) -> p c q", c=ND),
                            qfm.rearrange("(c p) q -> p c q", p=P)[
                                :, :, (qb + 1) * QB:(qb + 2) * QB])
                        st["qld"].inc(ins, qb + 1)
                        kv_load(qb + 1, 0)
                for qs in range(8):
                    fin_dmas(s, NQB - 1, qs)

            @block.tensor
            def _(t):
                def emit_av(pr):
                    gp = pr["gp"]
                    par = gp % 2
                    par3 = gp % 3
                    if pr["qs"] == 0:
                        st["vld"].wait_done(t, pr["gl"])
                    st["pt"].wait_done(t, gp)
                    if gp >= 2:
                        t.wait_ge(sm["oup"].h, gp - 1)
                    for dh in range(2):
                        for ksc in range(pr["nksc"]):
                            ins = t.matmul(
                                av_ps[par][:, dh * 512:(dh + 1) * 512],
                                pt_sb[:, par3 * KC + ksc * P:par3 * KC + (ksc + 1) * P],
                                vsl(pr["gl"], ksc, dh * 512, 512),
                                start=(ksc == 0), stop=(ksc == pr["nksc"] - 1))
                    sm["av"].inc(ins, 1)

                for pr in pairs:
                    gp = pr["gp"]
                    par = gp % 2
                    if pr["qs"] == 0:
                        st["kld"].wait_done(t, pr["gl"])
                        if pr["kc"] == 0:
                            st["qld"].wait_done(t, pr["qb"])
                    if gp >= 2:
                        t.wait_ge(sm["expd"].h, gp - 1)
                    for nh in range(pr["nh"]):
                        for dc in range(ND):
                            ins = t.matmul(
                                s_ps[par][:, nh * 512:(nh + 1) * 512],
                                qsl(pr["qb"], dc, pr["qs"]),
                                ksl(pr["gl"], dc, nh * 512, 512),
                                start=(dc == 0), stop=(dc == ND - 1))
                        if nh == 0 and pr["nh"] == 2:
                            sm["sch"].inc(ins, 1)
                    sm["sc"].inc(ins, 1)
                    if gp >= 2 and pairs[gp - 2]["qb"] == pr["qb"]:
                        emit_av(pairs[gp - 2])
                    if pr["kc"] == pr["qb"] and pr["qs"] == 7:
                        if gp >= 1 and pairs[gp - 1]["qb"] == pr["qb"]:
                            emit_av(pairs[gp - 1])
                        emit_av(pr)

            @block.gpsimd
            def _(g):
                chunks = []
                for e in range(8):
                    for h in range(2):
                        chunks.append((glp[:, e * 4096 + h * 2048:e * 4096 + (h + 1) * 2048],
                                       wg_d[e][:, h * 2048:(h + 1) * 2048]))
                        chunks.append((ulp[:, e * 4096 + h * 2048:e * 4096 + (h + 1) * 2048],
                                       wu_d[e][:, h * 2048:(h + 1) * 2048]))
                        chunks.append((dlp[e * 4096 + h * 2048:e * 4096 + (h + 1) * 2048, :],
                                       wd_d[e][h * 2048:(h + 1) * 2048, :]))
                NCH = len(chunks)
                assert 16 * NCH == WCAST_ALL
                n_p1 = 10
                items = []
                for i in range(NCH):
                    if i < n_p1:
                        items.append((-1000 + i, "cast", i))
                    else:
                        frac = (i - n_p1) / max(1, NCH - n_p1 - 1)
                        items.append((int((frac ** 1.3) * 0.8 * NPAIR), "cast", i))
                for pr in pairs:
                    if pr["diag"]:
                        items.append((pr["gp"] + 1, "mask", pr))
                items.sort(key=lambda x: (x[0], x[1] == "cast"))
                for key, kind, payload in items:
                    if kind == "cast":
                        i = payload
                        if i < n_p1:
                            g.wait_ge(sm["vmm"].h, min(8 * NG, 16 * i))
                        else:
                            g.wait_ge(sm["sc"].h, min(NPAIR, max(0, key)))
                        ins = g.dma_start(*chunks[i])
                        sm["wcast"].inc(ins, 16)
                    else:
                        pr = payload
                        gp, qs = pr["gp"], pr["qs"]
                        par3 = gp % 3
                        g.wait_ge(sm["expd"].h, gp + 1)
                        ins = g.affine_select(
                            p_sb[:, par3 * KC:par3 * KC + pr["W"]],
                            p_sb[:, par3 * KC:par3 * KC + pr["W"]],
                            [[-1, pr["W"]]], ALU.is_ge, 0.0,
                            base=qs * P, channel_multiplier=1)
                        sm["msk"].inc(ins, 1)

            def dve_finalize(v, qb, qs):
                fq = qb * 8 + qs
                par = fq % 2
                st["xres"].wait_done(v, fq)
                v.drain()
                v.reciprocal(sml[:, 38:39], l_sb[:, qs:qs + 1])
                v.drain()
                ins = v.scalar_tensor_tensor(
                    h_pre[:, :], o_acc[:, qs * D:(qs + 1) * D],
                    sml[:, 38:39], x_res[:, par * D:(par + 1) * D],
                    ALU.mult, ALU.add)
                sm["fdve"].inc(ins, 1)
                v.drain()
                v.scalar_tensor_tensor(
                    scr[:, :], h_pre[:, :], 1.0, h_pre[:, :],
                    ALU.mult, ALU.mult, accum_out=sml[:, 32:33])
                v.wait_ge(sm["fact"].h, 2 * fq + 1)
                v.tensor_scalar_mul(sml[:, 34:35], sml[:, 33:34], -1.0 / D)
                v.drain()
                v.tensor_tensor(sml[:, 35:36], sml[:, 34:35],
                                sml[:, 34:35], ALU.mult)
                v.drain()
                v.scalar_tensor_tensor(
                    sml[:, 35:36], sml[:, 32:33], 1.0 / D,
                    sml[:, 35:36], ALU.mult, ALU.subtract)
                v.drain()
                ins = v.tensor_scalar_add(sml[:, 35:36],
                                          sml[:, 35:36], EPS)
                sm["fdve"].inc(ins, 1)
                v.wait_ge(sm["fact"].h, 2 * fq + 2)
                if fq >= 2:
                    st["tokst"].wait_done(v, fq - 2)
                v.drain()
                ins = v.tensor_scalar(h1f32[:, par * D:(par + 1) * D],
                                      h_pre[:, :], sml[:, 34:35],
                                      sml[:, 36:37], ALU.add, ALU.mult)
                sm["fdve"].inc(ins, 1)
                if fq >= 2:
                    st["h1tr"].wait_done(v, fq - 2)
                v.drain()
                ins = v.tensor_copy(h1lp[:, par * D:(par + 1) * D],
                                    h1f32[:, par * D:(par + 1) * D])
                sm["fin"].inc(ins, 1)

            def cmax_part(v, pr):
                gp = pr["gp"]
                sb = (gp % 2) * 12
                first = pr["first"]
                if pr["nh"] == 2:
                    v.wait_ge(sm["sch"].h, cum_sch[gp + 1])
                    v.reduce_max(sml[:, sb:sb + 1],
                                 s_ps[gp % 2][:, 0:512], axis=AX)
                    v.wait_ge(sm["sc"].h, gp + 1)
                    v.reduce_max(sml[:, sb + 1:sb + 2],
                                 s_ps[gp % 2][:, 512:1024], axis=AX)
                    v.drain()
                    v.tensor_tensor(sml[:, sb:sb + 1], sml[:, sb:sb + 1],
                                    sml[:, sb + 1:sb + 2], ALU.max)
                else:
                    v.wait_ge(sm["sc"].h, gp + 1)
                    v.reduce_max(sml[:, sb:sb + 1],
                                 s_ps[gp % 2][:, 0:pr["W"]], axis=AX)
                v.drain()
                if first:
                    v.tensor_scalar_mul(sml[:, sb + 2:sb + 3],
                                        sml[:, sb:sb + 1], SCALE)
                else:
                    v.tensor_scalar(sml[:, sb + 2:sb + 3],
                                    sml[:, sb:sb + 1], SCALE,
                                    m_sb[:, pr["qs"]:pr["qs"] + 1],
                                    ALU.mult, ALU.max)
                v.drain()
                if not first:
                    v.tensor_tensor(sml[:, sb + 3:sb + 4],
                                    m_sb[:, pr["qs"]:pr["qs"] + 1],
                                    sml[:, sb + 2:sb + 3], ALU.subtract)
                v.tensor_scalar_mul(sml[:, sb + 4:sb + 5],
                                    sml[:, sb + 2:sb + 3], -1.0)
                v.drain()
                ins = v.tensor_copy(m_sb[:, pr["qs"]:pr["qs"] + 1],
                                    sml[:, sb + 2:sb + 3])
                sm["dve1"].inc(ins, 1)

            def o_part(v, pr):
                gp, qs = pr["gp"], pr["qs"]
                lc = 24 + (gp % 4) * 2
                first = pr["first"]
                if pr["diag"]:
                    v.wait_ge(sm["msk"].h, pr["di"] + 1)
                    v.drain()
                    v.reduce_sum(sml[:, lc:lc + 1],
                                 p_sb[:, (gp % 3) * KC:(gp % 3) * KC + pr["W"]],
                                 axis=AX)
                    v.drain()
                v.wait_ge(sm["av"].h, gp + 1)
                if first:
                    v.tensor_copy(l_sb[:, qs:qs + 1], sml[:, lc:lc + 1])
                    for dh in range(2):
                        ins = v.tensor_copy(
                            o_acc[:, qs * D + dh * 512:qs * D + (dh + 1) * 512],
                            av_ps[gp % 2][:, dh * 512:(dh + 1) * 512])
                else:
                    v.scalar_tensor_tensor(
                        l_sb[:, qs:qs + 1], l_sb[:, qs:qs + 1],
                        sml[:, lc + 1:lc + 2], sml[:, lc:lc + 1],
                        ALU.mult, ALU.add)
                    for dh in range(2):
                        ins = v.scalar_tensor_tensor(
                            o_acc[:, qs * D + dh * 512:qs * D + (dh + 1) * 512],
                            o_acc[:, qs * D + dh * 512:qs * D + (dh + 1) * 512],
                            sml[:, lc + 1:lc + 2],
                            av_ps[gp % 2][:, dh * 512:(dh + 1) * 512],
                            ALU.mult, ALU.add)
                sm["oup"].inc(ins, 1)

            @block.vector
            def _(v):
                for pr in pairs:
                    gp = pr["gp"]
                    if pr["kc"] == 0 and pr["qb"] >= 1:
                        dve_finalize(v, pr["qb"] - 1, pr["qs"])
                    cmax_part(v, pr)
                    if gp >= 2 and pairs[gp - 2]["qb"] == pr["qb"]:
                        o_part(v, pairs[gp - 2])
                    if pr["kc"] == pr["qb"] and pr["qs"] == 7:
                        if gp >= 1 and pairs[gp - 1]["qb"] == pr["qb"]:
                            o_part(v, pairs[gp - 1])
                        o_part(v, pr)
                for qs in range(8):
                    dve_finalize(v, NQB - 1, qs)

            def act_finalize(sc, qb, qs):
                fq = qb * 8 + qs
                sc.wait_ge(sm["fdve"].h, 3 * fq + 1)
                ins = sc.activation(scr_a[:, :], h_pre[:, :], AF.Copy,
                                    accum_out=sml[:, 33:34])
                sm["fact"].inc(ins, 1)
                sc.wait_ge(sm["fdve"].h, 3 * fq + 2)
                sc.activation(sml[:, 36:37], sml[:, 35:36], AF.Ln)
                sc.drain()
                ins = sc.activation(sml[:, 36:37], sml[:, 36:37],
                                    AF.Exp, scale=-0.5)
                sm["fact"].inc(ins, 1)

            @block.scalar
            def _(sc):
                for qb in range(NQB):
                    for kc in range(qb + 1):
                        gl = gl_of(qb, kc)
                        for qs in range(8):
                            if kc == 0 and qb >= 1:
                                act_finalize(sc, qb - 1, qs)
                            pr = pairs[gl * 8 + qs]
                            gp = pr["gp"]
                            sb = (gp % 2) * 12
                            sc.wait_ge(sm["dve1"].h, gp + 1)
                            lc = 24 + (gp % 4) * 2
                            if not pr["first"]:
                                sc.activation(sml[:, lc + 1:lc + 2],
                                              sml[:, sb + 3:sb + 4], AF.Exp)
                                sc.drain()
                            if gp >= 3:
                                st["pt"].wait_done(sc, gp - 3)
                                sc.wait_ge(sm["oup"].h, gp - 2)
                            ins = sc.activation(
                                p_sb[:, (gp % 3) * KC:(gp % 3) * KC + pr["W"]],
                                s_ps[gp % 2][:, 0:pr["W"]], AF.Exp,
                                bias=sml[:, sb + 4:sb + 5], scale=SCALE,
                                accum_out=(None if pr["diag"]
                                           else sml[:, lc:lc + 1]))
                            sm["expd"].inc(ins, 1)
                for qs in range(8):
                    act_finalize(sc, NQB - 1, qs)

        with ExitStack() as p4:
            e4 = p4.enter_context
            h1_sb = e4(nc.sbuf_tensor("h1_sb", [P, 2 * ND * TS_T], BF16))
            wg_sb = e4(nc.sbuf_tensor("wg_sb", [P, 2 * ND * FCB * P], BF16))
            wu_sb = e4(nc.sbuf_tensor("wu_sb", [P, 2 * ND * FCB * P], BF16))
            wd_sb = e4(nc.sbuf_tensor("wd_sb", [P, 2 * FCB * D], BF16))
            a_silu = e4(nc.sbuf_tensor("a_silu", [P, 2 * TS_T], BF16))
            a_sb = e4(nc.sbuf_tensor("a_sb", [P, 2 * TS_T], BF16))
            h1res = e4(nc.sbuf_tensor("h1res", [P, 2 * 3 * D], F32))
            s_pre = e4(nc.sbuf_tensor("s_pre", [P, 3 * D], F32))
            scr4 = e4(nc.sbuf_tensor("scr4", [P, D], F32))
            scr4a = e4(nc.sbuf_tensor("scr4a", [P, D], F32))
            out_sb = e4(nc.sbuf_tensor("out_sb", [P, 4 * D], F32))
            sml4 = e4(nc.sbuf_tensor("sml4", [P, 24], F32))
            g_ps = e4(nc.psum_tensor("g_ps", [P, TS_T], F32))
            u_ps = e4(nc.psum_tensor("u_ps", [P, TS_T], F32))
            y_ps = [e4(nc.psum_tensor(f"y_ps{i}", [P, 512], F32)) for i in range(6)]

            fq2_of_ts = []
            acc = 0
            for (t0v, T) in ts_list:
                fq2_of_ts.append(acc)
                acc += T // P

            def wg_slice(gfcb, dc, fcl):
                base = (gfcb % 2) * ND * FCB * P + dc * FCB * P
                return wg_sb[:, base + fcl * P:base + (fcl + 1) * P]

            def wu_slice(gfcb, dc, fcl):
                base = (gfcb % 2) * ND * FCB * P + dc * FCB * P
                return wu_sb[:, base + fcl * P:base + (fcl + 1) * P]

            def wd_slice(gfcb, fcl, dh):
                base = (gfcb % 2) * FCB * D + fcl * D
                return wd_sb[:, base + dh * 512:base + (dh + 1) * 512]

            def p4_stores(s, ts):
                t0v, T = ts_list[ts]
                for tc in range(T // P):
                    fq2 = fq2_of_ts[ts] + tc
                    s.wait_ge(sm["h2dve"].h, 3 * fq2 + 3)
                    ins = s.dma_start(
                        out_d[t0v + tc * P:t0v + (tc + 1) * P, :],
                        out_sb[:, (fq2 % 4) * D:(fq2 % 4 + 1) * D])
                    st["outst"].inc(ins, fq2)

            @block.sync
            def _(s):
                s.wait_ge(sm["wcast"].h, WCAST_ALL)
                st["h1st"].wait_all(s)
                for ts in range(NTS):
                    for fcb in range(NFCB):
                        if fcb == NFCB // 2 and ts >= 1:
                            p4_stores(s, ts - 1)
                        gfcb = ts * NFCB + fcb
                        if gfcb >= 2:
                            s.wait_ge(sm["gmm"].h, 4 * (gfcb - 1))
                        ins = s.dma_start(
                            wg_sb[:, (gfcb % 2) * ND * FCB * P:
                                  ((gfcb % 2) + 1) * ND * FCB * P]
                            .rearrange("p (c f) -> p c f", c=ND),
                            glp.rearrange("(c p) f -> p c f", p=P)[
                                :, :, fcb * FCB * P:(fcb + 1) * FCB * P])
                        st["wgld"].inc(ins, gfcb)
                        if gfcb >= 2:
                            s.wait_ge(sm["umm"].h, 4 * (gfcb - 1))
                        ins = s.dma_start(
                            wu_sb[:, (gfcb % 2) * ND * FCB * P:
                                  ((gfcb % 2) + 1) * ND * FCB * P]
                            .rearrange("p (c f) -> p c f", c=ND),
                            ulp.rearrange("(c p) f -> p c f", p=P)[
                                :, :, fcb * FCB * P:(fcb + 1) * FCB * P])
                        st["wuld"].inc(ins, gfcb)
                p4_stores(s, NTS - 1)
                st["outst"].wait_all(s)

            def h1_res_load(g, ts):
                t0v, T = ts_list[ts]
                if ts >= 2:
                    g.wait_ge(sm["umm"].h, NFC * (ts - 1))
                ins = g.dma_start(
                    h1_sb[:, (ts % 2) * ND * TS_T:(ts % 2) * ND * TS_T + ND * T]
                    .rearrange("p (c t) -> p c t", c=ND),
                    h1fm.rearrange("(c p) t -> p c t", p=P)[:, :, t0v:t0v + T])
                st["h1ld"].inc(ins, ts)
                if ts >= 2:
                    g.wait_ge(sm["h2ev"].h, ts - 1)
                ins = g.dma_start(
                    h1res[:, (ts % 2) * 3 * D:(ts % 2) * 3 * D + (T // P) * D]
                    .rearrange("p (tc d) -> p tc d", tc=T // P),
                    h1tok.rearrange("(tc p) d -> tc p d", p=P)[
                        t0v // P:t0v // P + T // P].rearrange("tc p d -> p tc d"))
                st["resld"].inc(ins, ts)

            @block.gpsimd
            def _(g):
                g.wait_ge(sm["wcast"].h, WCAST_ALL)
                st["tokst"].wait_all(g)
                st["h1st"].wait_all(g)
                h1_res_load(g, 0)
                for ts in range(NTS):
                    t0v, T = ts_list[ts]
                    if ts + 1 < NTS:
                        h1_res_load(g, ts + 1)
                    for fcb in range(NFCB):
                        gfcb = ts * NFCB + fcb
                        if gfcb >= 2:
                            g.wait_ge(sm["ydone"].h, 4 * (gfcb - 1))
                        ins = g.dma_start(
                            wd_sb[:, (gfcb % 2) * FCB * D:((gfcb % 2) + 1) * FCB * D]
                            .rearrange("p (c d) -> p c d", c=FCB),
                            dlp.rearrange("(c p) d -> p c d", p=P)[
                                :, fcb * FCB:(fcb + 1) * FCB, :])
                        st["wdld"].inc(ins, gfcb)

            @block.tensor
            def _(t):
                def h1sl(ts, dc, T):
                    base = (ts % 2) * ND * TS_T + dc * T
                    return h1_sb[:, base:base + T]

                def y_mms(ts, fc, T, start, stop):
                    apar = (ts * NFC + fc) % 2
                    gfcb = ts * NFCB + fc // FCB
                    for tc in range(T // P):
                        for dh in range(2):
                            ins = t.matmul(
                                y_ps[tc * 2 + dh][:, :],
                                a_sb[:, apar * TS_T + tc * P:apar * TS_T + (tc + 1) * P],
                                wd_slice(gfcb, fc % FCB, dh),
                                start=start, stop=stop)
                    return ins

                for ts in range(NTS):
                    t0v, T = ts_list[ts]
                    st["h1ld"].wait_done(t, ts)
                    for fc in range(NFC):
                        gfc = ts * NFC + fc
                        gfcb = ts * NFCB + fc // FCB
                        if fc % FCB == 0:
                            st["wgld"].wait_done(t, gfcb)
                            st["wuld"].wait_done(t, gfcb)
                        if gfc >= 1:
                            t.wait_ge(sm["silu"].h, gfc)
                        for dc in range(ND):
                            ins = t.matmul(g_ps[:, 0:T],
                                           wg_slice(gfcb, dc, fc % FCB),
                                           h1sl(ts, dc, T),
                                           start=(dc == 0), stop=(dc == ND - 1))
                        sm["gmm"].inc(ins, 1)
                        if gfc >= 1:
                            t.wait_ge(sm["mul"].h, gfc)
                        for dc in range(ND):
                            ins = t.matmul(u_ps[:, 0:T],
                                           wu_slice(gfcb, dc, fc % FCB),
                                           h1sl(ts, dc, T),
                                           start=(dc == 0), stop=(dc == ND - 1))
                        sm["umm"].inc(ins, 1)
                        if fc >= 1:
                            fp = fc - 1
                            if fp % FCB == 0:
                                st["wdld"].wait_done(t, ts * NFCB + fp // FCB)
                            t.wait_ge(sm["mul"].h, gfc)
                            if fc == 1 and ts >= 1:
                                t.wait_ge(sm["h2ev"].h, ts)
                            ins = y_mms(ts, fp, T, start=(fp == 0), stop=False)
                            sm["ydone"].inc(ins, 1)
                    t.wait_ge(sm["mul"].h, ts * NFC + NFC)
                    ins = y_mms(ts, NFC - 1, T, start=False, stop=True)
                    sm["ydone"].inc(ins, 1)

            def ln2_lsum(sc, ts, tc):
                fq2 = fq2_of_ts[ts] + tc
                q = (fq2 % 3) * 8
                sc.wait_ge(sm["h2dve"].h, 3 * fq2 + 1)
                ins = sc.activation(scr4a[:, :],
                                    s_pre[:, (fq2 % 3) * D:(fq2 % 3 + 1) * D],
                                    AF.Copy, accum_out=sml4[:, q + 1:q + 2])
                sm["h2act"].inc(ins, 1)

            def ln2_rstd(sc, ts, tc):
                fq2 = fq2_of_ts[ts] + tc
                q = (fq2 % 3) * 8
                sc.wait_ge(sm["h2dve"].h, 3 * fq2 + 2)
                sc.activation(sml4[:, q + 4:q + 5], sml4[:, q + 3:q + 4], AF.Ln)
                sc.drain()
                ins = sc.activation(sml4[:, q + 4:q + 5], sml4[:, q + 4:q + 5],
                                    AF.Exp, scale=-0.5)
                sm["h2act"].inc(ins, 1)

            @block.scalar
            def _(sc):
                for ts in range(NTS):
                    t0v, T = ts_list[ts]
                    hooks = {}
                    if ts >= 1:
                        pT = ts_list[ts - 1][1]
                        for tc in range(pT // P):
                            hooks[14 + 40 * tc] = (ln2_lsum, tc)
                            hooks[30 + 40 * tc] = (ln2_rstd, tc)
                    for fc in range(NFC):
                        gfc = ts * NFC + fc
                        par = gfc % 2
                        sc.wait_ge(sm["gmm"].h, gfc + 1)
                        if gfc >= 2:
                            sc.wait_ge(sm["mul"].h, gfc - 1)
                        ins = sc.activation(a_silu[:, par * TS_T:par * TS_T + T],
                                            g_ps[:, 0:T], AF.Silu)
                        sm["silu"].inc(ins, 1)
                        if fc in hooks:
                            fn, tc = hooks[fc]
                            fn(sc, ts - 1, tc)
                for tc in range(ts_list[NTS - 1][1] // P):
                    ln2_lsum(sc, NTS - 1, tc)
                    ln2_rstd(sc, NTS - 1, tc)

            def ln2_adds(v, ts, tc):
                fq2 = fq2_of_ts[ts] + tc
                sp = (fq2 % 3) * D
                rpar = ts % 2
                T = ts_list[ts][1]
                if tc == 0:
                    v.wait_ge(sm["ydone"].h, NFC * (ts + 1))
                    st["resld"].wait_done(v, ts)
                v.drain()
                for dh in range(2):
                    ins = v.tensor_tensor(
                        s_pre[:, sp + dh * 512:sp + (dh + 1) * 512],
                        y_ps[tc * 2 + dh][:, :],
                        h1res[:, rpar * 3 * D + tc * D + dh * 512:
                              rpar * 3 * D + tc * D + (dh + 1) * 512],
                        ALU.add)
                if tc == T // P - 1:
                    sm["h2ev"].inc(ins, 1)

            def ln2_ssq(v, ts, tc):
                fq2 = fq2_of_ts[ts] + tc
                sp = (fq2 % 3) * D
                q = (fq2 % 3) * 8
                v.drain()
                ins = v.scalar_tensor_tensor(
                    scr4[:, :], s_pre[:, sp:sp + D], 1.0, s_pre[:, sp:sp + D],
                    ALU.mult, ALU.mult, accum_out=sml4[:, q:q + 1])
                sm["h2dve"].inc(ins, 1)

            def ln2_meanvar(v, ts, tc):
                fq2 = fq2_of_ts[ts] + tc
                q = (fq2 % 3) * 8
                v.wait_ge(sm["h2act"].h, 2 * fq2 + 1)
                v.drain()
                v.tensor_scalar_mul(sml4[:, q + 2:q + 3], sml4[:, q + 1:q + 2],
                                    -1.0 / D)
                v.drain()
                v.tensor_tensor(sml4[:, q + 3:q + 4], sml4[:, q + 2:q + 3],
                                sml4[:, q + 2:q + 3], ALU.mult)
                v.drain()
                v.scalar_tensor_tensor(
                    sml4[:, q + 3:q + 4], sml4[:, q:q + 1], 1.0 / D,
                    sml4[:, q + 3:q + 4], ALU.mult, ALU.subtract)
                v.drain()
                ins = v.tensor_scalar_add(sml4[:, q + 3:q + 4],
                                          sml4[:, q + 3:q + 4], EPS)
                sm["h2dve"].inc(ins, 1)

            def ln2_apply(v, ts, tc):
                fq2 = fq2_of_ts[ts] + tc
                sp = (fq2 % 3) * D
                q = (fq2 % 3) * 8
                v.wait_ge(sm["h2act"].h, 2 * fq2 + 2)
                if fq2 >= 4:
                    st["outst"].wait_done(v, fq2 - 4)
                v.drain()
                ins = v.tensor_scalar(
                    out_sb[:, (fq2 % 4) * D:(fq2 % 4 + 1) * D],
                    s_pre[:, sp:sp + D], sml4[:, q + 2:q + 3],
                    sml4[:, q + 4:q + 5], ALU.add, ALU.mult)
                sm["h2dve"].inc(ins, 1)

            @block.vector
            def _(v):
                for ts in range(NTS):
                    t0v, T = ts_list[ts]
                    hooks = {}
                    if ts >= 1:
                        pT = ts_list[ts - 1][1]
                        for tc in range(pT // P):
                            ln2_adds(v, ts - 1, tc)
                            hooks[8 + 40 * tc] = (ln2_ssq, tc)
                            hooks[24 + 40 * tc] = (ln2_meanvar, tc)
                            hooks[40 + 40 * tc] = (ln2_apply, tc)
                    for fc in range(NFC):
                        gfc = ts * NFC + fc
                        par = gfc % 2
                        v.wait_ge(sm["silu"].h, gfc + 1)
                        v.wait_ge(sm["umm"].h, gfc + 1)
                        if gfc >= 2:
                            v.wait_ge(sm["ydone"].h, gfc - 1)
                        ins = v.tensor_tensor(a_sb[:, par * TS_T:par * TS_T + T],
                                              a_silu[:, par * TS_T:par * TS_T + T],
                                              u_ps[:, 0:T], ALU.mult)
                        sm["mul"].inc(ins, 1)
                        if fc in hooks:
                            fn, tc = hooks[fc]
                            fn(v, ts - 1, tc)
                for tc in range(ts_list[NTS - 1][1] // P):
                    ln2_adds(v, NTS - 1, tc)
                    ln2_ssq(v, NTS - 1, tc)
                    ln2_meanvar(v, NTS - 1, tc)
                    ln2_apply(v, NTS - 1, tc)

    return nc


def kernel(L0_attn_x, L0_attn_Qw, L0_attn_Kw, L0_attn_Vw,
           L0_ffn_Wg, L0_ffn_Wu, L0_ffn_Wd):
    x = np.ascontiguousarray(np.asarray(L0_attn_x, dtype=np.float32))
    qw = np.ascontiguousarray(np.asarray(L0_attn_Qw, dtype=np.float32))
    kw = np.ascontiguousarray(np.asarray(L0_attn_Kw, dtype=np.float32))
    vw = np.ascontiguousarray(np.asarray(L0_attn_Vw, dtype=np.float32))
    wg = np.ascontiguousarray(np.asarray(L0_ffn_Wg, dtype=np.float32))
    wu = np.ascontiguousarray(np.asarray(L0_ffn_Wu, dtype=np.float32))
    wd = np.ascontiguousarray(np.asarray(L0_ffn_Wd, dtype=np.float32))
    B, S, _ = x.shape
    nc = build(S)
    in_maps = [dict(x=x[b], qw=qw, kw=kw, vw=vw, wg=wg, wu=wu, wd=wd)
               for b in range(B)]
    res = run_bass_kernel_spmd(nc, in_maps, core_ids=list(range(B)))
    return np.stack([r["out"] for r in res.results], axis=0)


# version 10
# speedup vs baseline: 1.2408x; 1.1487x over previous
import numpy as np
from contextlib import ExitStack

import concourse.bass as bass
import concourse.mybir as mybir
from concourse.bass_utils import run_bass_kernel_spmd

F32 = mybir.dt.float32
BF16 = mybir.dt.bfloat16
AF = mybir.ActivationFunctionType
ALU = mybir.AluOpType
AX = mybir.AxisListType.X

P = 128
D = 1024
ND = D // P
FF = 32768
NFC = FF // P
FCB = 4
NFCB = NFC // FCB
TS_T = 384
QB = 1024
KC = 1024
G1 = 512
NTC = G1 // P
EPS = 1e-5
SCALE = 1.0 / 32.0


class Sem:
    def __init__(self, nc, name):
        self.h = nc.alloc_semaphore(name)
        self.n = 0

    def inc(self, ins, k=1):
        ins.then_inc(self.h, k)
        self.n += k


class DmaStream:

    def __init__(self, nc, name, nbufs=2):
        self.nb = nbufs
        self.s = [Sem(nc, name + str(i)) for i in range(nbufs)]
        self.cum = {}

    def inc(self, ins, i):
        s = self.s[i % self.nb]
        s.inc(ins, 16)
        self.cum[i] = s.n

    def wait_done(self, eng, i):
        eng.wait_ge(self.s[i % self.nb].h, self.cum[i])

    def wait_count(self, eng, i, per_index):
        eng.wait_ge(self.s[i % self.nb].h, 16 * per_index * (i // self.nb + 1))

    def wait_all(self, eng):
        for s in self.s:
            if s.n:
                eng.wait_ge(s.h, s.n)


def build(S=8192):
    nc = bass.Bass("TRN2", target_bir_lowering=False)

    x_d = nc.dram_tensor("x", [S, D], F32, kind="ExternalInput").ap()
    qw_d = nc.dram_tensor("qw", [D, D], F32, kind="ExternalInput").ap()
    kw_d = nc.dram_tensor("kw", [D, D], F32, kind="ExternalInput").ap()
    vw_d = nc.dram_tensor("vw", [D, D], F32, kind="ExternalInput").ap()
    wg_d = nc.dram_tensor("wg", [8, D, 4096], F32, kind="ExternalInput").ap()
    wu_d = nc.dram_tensor("wu", [8, D, 4096], F32, kind="ExternalInput").ap()
    wd_d = nc.dram_tensor("wd", [8, 4096, D], F32, kind="ExternalInput").ap()
    out_d = nc.dram_tensor("out", [S, D], F32, kind="ExternalOutput").ap()

    glp = nc.dram_tensor("glp", [D, FF], BF16, kind="Internal").ap()
    ulp = nc.dram_tensor("ulp", [D, FF], BF16, kind="Internal").ap()
    dlp = nc.dram_tensor("dlp", [FF, D], BF16, kind="Internal").ap()
    qfm = nc.dram_tensor("qfm", [D, S], BF16, kind="Internal").ap()
    kfm = nc.dram_tensor("kfm", [D, S], BF16, kind="Internal").ap()
    vtok = nc.dram_tensor("vtok", [S, D], BF16, kind="Internal").ap()
    h1tok = nc.dram_tensor("h1tok", [S, D], F32, kind="Internal").ap()
    h1fm = nc.dram_tensor("h1fm", [D, S], BF16, kind="Internal").ap()

    NG = S // G1
    NQB = S // QB
    ts_list = []
    t0 = 0
    while t0 < S:
        t = min(TS_T, S - t0)
        ts_list.append((t0, t))
        t0 += t
    if len(ts_list) >= 2 and ts_list[-1][1] == 128 and ts_list[-2][1] == 384:
        base = ts_list[-2][0]
        ts_list[-2] = (base, 256)
        ts_list[-1] = (base + 256, 256)
    NTS = len(ts_list)

    def gl_of(qb, kc):
        return qb * (qb + 1) // 2 + kc

    pairs = []
    ndiag = 0
    for qb in range(NQB):
        for kc in range(qb + 1):
            for qs in range(8):
                diag = kc == qb
                if diag:
                    nv = (qs + 1) * P
                    nh = (nv + 511) // 512
                    nksc = qs + 1
                    di = ndiag
                    ndiag += 1
                else:
                    nh, nksc, di = 2, 8, None
                pairs.append(dict(qb=qb, kc=kc, qs=qs, gl=gl_of(qb, kc),
                                  gp=len(pairs), diag=diag, nh=nh,
                                  nksc=nksc, W=nh * 512, di=di,
                                  first=(kc == 0)))
    NPAIR = len(pairs)
    cum_sch = [0]
    for pr in pairs:
        cum_sch.append(cum_sch[-1] + (1 if pr["nh"] == 2 else 0))

    with ExitStack() as top:
        sm = {k: Sem(nc, k) for k in [
            "wcast", "qkvw", "qmm", "kmm", "vmm", "qev", "kev", "vev",
            "sc", "dve1", "expd", "msk", "av", "oup", "fdve", "fact", "fin",
            "gmm", "umm", "silu", "mul", "ydone", "h2dve", "h2act", "h2ev",
            "xc", "sch",
        ]}
        st = {k: DmaStream(nc, k) for k in [
            "xlp", "xtr", "qst", "kst", "vst",
            "qld", "kld", "vld", "xres", "tokst", "h1tr", "h1st",
            "h1ld", "resld",
        ]}
        st["pt"] = DmaStream(nc, "pt", nbufs=3)
        for k in ("wgld", "wuld", "wdld"):
            st[k] = DmaStream(nc, k, nbufs=3)
        st["outst"] = DmaStream(nc, "outst", nbufs=4)
        block = top.enter_context(nc.Block())

        with ExitStack() as p1:
            e1 = p1.enter_context
            qw_sb = e1(nc.sbuf_tensor("qw_sb", [P, ND * D], BF16))
            kw_sb = e1(nc.sbuf_tensor("kw_sb", [P, ND * D], BF16))
            vw_sb = e1(nc.sbuf_tensor("vw_sb", [P, ND * D], BF16))
            x_lp = e1(nc.sbuf_tensor("x_lp", [P, 2 * NTC * D], BF16))
            x_f32 = e1(nc.sbuf_tensor("x_f32", [P, 2 * NTC * D], F32))
            x_fm = e1(nc.sbuf_tensor("x_fm", [P, 2 * ND * G1], BF16))
            q_stg = e1(nc.sbuf_tensor("q_stg", [P, 2 * 8 * 512], BF16))
            k_stg = e1(nc.sbuf_tensor("k_stg", [P, 2 * 8 * 512], BF16))
            v_stg = e1(nc.sbuf_tensor("v_stg", [P, 2 * 8 * 512], BF16))
            q_ps = [e1(nc.psum_tensor(f"q_ps{i}", [P, 512], F32)) for i in range(2)]
            k_ps = [e1(nc.psum_tensor(f"k_ps{i}", [P, 512], F32)) for i in range(2)]
            v_ps = [e1(nc.psum_tensor(f"v_ps{i}", [P, 512], F32)) for i in range(2)]

            def xfm(par, dc, lo, n):
                base = par * ND * G1 + dc * G1
                return x_fm[:, base + lo:base + lo + n]

            @block.gpsimd
            def _(g):
                for w_sb, w_dd in ((qw_sb, qw_d), (kw_sb, kw_d), (vw_sb, vw_d)):
                    ins = g.dma_start(
                        w_sb[:, :].rearrange("p (c o) -> p c o", c=ND),
                        w_dd.rearrange("(c p) o -> p c o", p=P))
                    sm["qkvw"].inc(ins, 16)

            QKVW_ALL = sm["qkvw"].n
            WCAST_ALL = 16 * 48

            def p1_stores(s, gi):
                par = gi % 2
                for stg, ev, store_st, dst in (
                        (q_stg, sm["qev"], st["qst"], qfm),
                        (k_stg, sm["kev"], st["kst"], kfm)):
                    s.wait_ge(ev.h, 8 * (gi + 1))
                    ins = s.dma_start(
                        dst.rearrange("(oc p) s -> p oc s", p=P)[
                            :, :, gi * G1:(gi + 1) * G1],
                        stg[:, par * 8 * 512:(par + 1) * 8 * 512]
                        .rearrange("p (oc s) -> p oc s", oc=ND))
                    store_st.inc(ins, gi)
                s.wait_ge(sm["vev"].h, 8 * (gi + 1))
                ins = s.dma_start(
                    vtok[gi * G1:(gi + 1) * G1, :]
                    .rearrange("(tc p) d -> p tc d", p=P),
                    v_stg[:, par * 8 * 512:(par + 1) * 8 * 512]
                    .rearrange("p (tc d) -> p tc d", tc=NTC))
                st["vst"].inc(ins, gi)

            def x_load(s, gi):
                par = gi % 2
                if gi >= 2:
                    s.wait_ge(sm["xc"].h, gi - 1)
                ins = s.dma_start(
                    x_f32[:, par * NTC * D:(par + 1) * NTC * D]
                    .rearrange("p (tc d) -> p tc d", tc=NTC),
                    x_d.rearrange("(tc p) d -> tc p d", p=P)[
                        gi * NTC:(gi + 1) * NTC].rearrange("tc p d -> p tc d"))
                st["xlp"].inc(ins, gi)

            @block.sync
            def _(s):
                x_load(s, 0)
                for gi in range(NG):
                    par = gi % 2
                    if gi + 1 < NG:
                        x_load(s, gi + 1)
                    s.wait_ge(sm["xc"].h, gi + 1)
                    if gi >= 2:
                        s.wait_ge(sm["vmm"].h, 8 * (gi - 1))
                    for tc in range(NTC):
                        dst = x_fm[:, par * ND * G1:(par + 1) * ND * G1] \
                            .rearrange("p (dc g) -> p dc g", dc=ND)[
                                :, :, tc * P:(tc + 1) * P]
                        srcv = x_lp[:, (par * NTC + tc) * D:
                                    (par * NTC + tc + 1) * D] \
                            .rearrange("p (dc c) -> p dc c", dc=ND)
                        ins = s.dma_start(dst, srcv, transpose=True)
                        st["xtr"].inc(ins, gi)
                    if gi >= 1:
                        p1_stores(s, gi - 1)
                p1_stores(s, NG - 1)

            @block.tensor
            def _(t):
                t.wait_ge(sm["qkvw"].h, QKVW_ALL)
                for gi in range(NG):
                    par = gi % 2
                    st["xtr"].wait_done(t, gi)
                    for w_sb, ps, mm, ev in ((qw_sb, q_ps, sm["qmm"], sm["qev"]),
                                             (kw_sb, k_ps, sm["kmm"], sm["kev"])):
                        for oc in range(ND):
                            j = gi * ND + oc
                            if j >= 2:
                                t.wait_ge(ev.h, j - 1)
                            for dc in range(ND):
                                ins = t.matmul(
                                    ps[j % 2][:, :],
                                    w_sb[:, dc * D + oc * P:dc * D + (oc + 1) * P],
                                    xfm(par, dc, 0, G1),
                                    start=(dc == 0), stop=(dc == ND - 1))
                            mm.inc(ins, 1)
                    for tc in range(NTC):
                        for dh in range(2):
                            j = gi * ND + tc * 2 + dh
                            if j >= 2:
                                t.wait_ge(sm["vev"].h, j - 1)
                            for dc in range(ND):
                                ins = t.matmul(
                                    v_ps[j % 2][:, :],
                                    xfm(par, dc, tc * P, P),
                                    vw_sb[:, dc * D + dh * 512:dc * D + (dh + 1) * 512],
                                    start=(dc == 0), stop=(dc == ND - 1))
                            sm["vmm"].inc(ins, 1)

            @block.scalar
            def _(sc):
                for gi in range(NG):
                    par = gi % 2
                    for ps, stg, mm, ev, store_st in (
                            (q_ps, q_stg, sm["qmm"], sm["qev"], st["qst"]),
                            (k_ps, k_stg, sm["kmm"], sm["kev"], st["kst"])):
                        if gi >= 2:
                            store_st.wait_done(sc, gi - 2)
                        for oc in range(ND):
                            j = gi * ND + oc
                            sc.wait_ge(mm.h, j + 1)
                            ins = sc.activation(
                                stg[:, (par * 8 + oc) * 512:(par * 8 + oc + 1) * 512],
                                ps[j % 2][:, :], AF.Copy)
                            ev.inc(ins, 1)

            @block.vector
            def _(v):
                for gi in range(NG):
                    par = gi % 2
                    st["xlp"].wait_done(v, gi)
                    if gi >= 2:
                        st["xtr"].wait_done(v, gi - 2)
                    ins = v.tensor_copy(
                        x_lp[:, par * NTC * D:(par + 1) * NTC * D],
                        x_f32[:, par * NTC * D:(par + 1) * NTC * D])
                    sm["xc"].inc(ins, 1)
                    if gi >= 2:
                        st["vst"].wait_done(v, gi - 2)
                    for tc in range(NTC):
                        for dh in range(2):
                            j = gi * ND + tc * 2 + dh
                            v.wait_ge(sm["vmm"].h, j + 1)
                            ins = v.tensor_copy(
                                v_stg[:, (par * 8 + tc * 2 + dh) * 512:
                                      (par * 8 + tc * 2 + dh + 1) * 512],
                                v_ps[j % 2][:, :])
                            sm["vev"].inc(ins, 1)

        with ExitStack() as p2:
            e2 = p2.enter_context
            q_sb = e2(nc.sbuf_tensor("q_sb", [P, 2 * ND * QB], BF16))
            k_sb = e2(nc.sbuf_tensor("k_sb", [P, 2 * ND * KC], BF16))
            v_sb = e2(nc.sbuf_tensor("v_sb", [P, 2 * (KC // P) * D], BF16))
            o_acc = e2(nc.sbuf_tensor("o_acc", [P, 8 * D], F32))
            p_sb = e2(nc.sbuf_tensor("p_sb", [P, 3 * KC], BF16))
            pt_sb = e2(nc.sbuf_tensor("pt_sb", [P, 3 * KC], BF16))
            m_sb = e2(nc.sbuf_tensor("m_sb", [P, 8], F32))
            l_sb = e2(nc.sbuf_tensor("l_sb", [P, 8], F32))
            sml = e2(nc.sbuf_tensor("sml", [P, 40], F32))
            x_res = e2(nc.sbuf_tensor("x_res", [P, 2 * D], F32))
            h_pre = e2(nc.sbuf_tensor("h_pre", [P, D], F32))
            scr = e2(nc.sbuf_tensor("scr", [P, D], F32))
            scr_a = e2(nc.sbuf_tensor("scr_a", [P, D], F32))
            h1f32 = e2(nc.sbuf_tensor("h1f32", [P, 2 * D], F32))
            h1lp = e2(nc.sbuf_tensor("h1lp", [P, 2 * D], BF16))
            h1fmt = e2(nc.sbuf_tensor("h1fmt", [P, 2 * D], BF16))
            s_ps = [e2(nc.psum_tensor(f"s_ps{i}", [P, KC], F32)) for i in range(2)]
            av_ps = [e2(nc.psum_tensor(f"av_ps{i}", [P, D], F32)) for i in range(2)]

            def qsl(qb, dc, qs):
                base = (qb % 2) * ND * QB + dc * QB
                return q_sb[:, base + qs * P:base + (qs + 1) * P]

            def ksl(gl, dc, lo, n):
                base = (gl % 2) * ND * KC + dc * KC
                return k_sb[:, base + lo:base + lo + n]

            def vsl(gl, ksc, lo, n):
                base = (gl % 2) * (KC // P) * D + ksc * D
                return v_sb[:, base + lo:base + lo + n]

            @block.sync
            def _(s):
                st["qst"].wait_all(s)
                st["kst"].wait_all(s)
                st["vst"].wait_all(s)
                def kv_load(qb, kc):
                    gl = gl_of(qb, kc)
                    if gl >= 2:
                        s.wait_ge(sm["sc"].h, 8 * (gl - 1))
                    ins = s.dma_start(
                        k_sb[:, (gl % 2) * ND * KC:((gl % 2) + 1) * ND * KC]
                        .rearrange("p (c k) -> p c k", c=ND),
                        kfm.rearrange("(c p) k -> p c k", p=P)[
                            :, :, kc * KC:(kc + 1) * KC])
                    st["kld"].inc(ins, gl)
                    if gl >= 2:
                        s.wait_ge(sm["av"].h, 8 * (gl - 1))
                    ins = s.dma_start(
                        v_sb[:, (gl % 2) * (KC // P) * D:((gl % 2) + 1) * (KC // P) * D]
                        .rearrange("p (k d) -> p k d", k=KC // P),
                        vtok.rearrange("(k p) d -> k p d", p=P)[
                            kc * (KC // P):(kc + 1) * (KC // P)]
                        .rearrange("k p d -> p k d"))
                    st["vld"].inc(ins, gl)

                def fin_dmas(s, qb, qs):
                    fq = qb * 8 + qs
                    par = fq % 2
                    row = qb * QB + qs * P
                    if fq >= 2:
                        s.wait_ge(sm["fdve"].h, 3 * (fq - 2) + 1)
                    ins = s.dma_start(x_res[:, par * D:(par + 1) * D],
                                      x_d[row:row + P, :])
                    st["xres"].inc(ins, fq)
                    s.wait_ge(sm["fdve"].h, 3 * fq + 3)
                    ins = s.dma_start(h1tok[row:row + P, :],
                                      h1f32[:, par * D:(par + 1) * D])
                    st["tokst"].inc(ins, fq)
                    s.wait_ge(sm["fin"].h, fq + 1)
                    if fq >= 2:
                        st["h1st"].wait_done(s, fq - 2)
                    ins = s.dma_start(
                        h1fmt[:, par * D:(par + 1) * D]
                        .rearrange("p (dc c) -> p dc c", dc=ND),
                        h1lp[:, par * D:(par + 1) * D]
                        .rearrange("p (dc c) -> p dc c", dc=ND),
                        transpose=True)
                    st["h1tr"].inc(ins, fq)
                    st["h1tr"].wait_done(s, fq)
                    ins = s.dma_start(
                        h1fm.rearrange("(c p) t -> p c t", p=P)[:, :, row:row + P],
                        h1fmt[:, par * D:(par + 1) * D]
                        .rearrange("p (c t) -> p c t", c=ND))
                    st["h1st"].inc(ins, fq)

                for qb in range(NQB):
                    if qb == 0:
                        ins = s.dma_start(
                            q_sb[:, 0:ND * QB].rearrange("p (c q) -> p c q", c=ND),
                            qfm.rearrange("(c p) q -> p c q", p=P)[:, :, 0:QB])
                        st["qld"].inc(ins, 0)
                        kv_load(0, 0)
                    for kc in range(qb + 1):
                        gl = gl_of(qb, kc)
                        if kc + 1 <= qb:
                            kv_load(qb, kc + 1)
                        for qs in range(8):
                            if kc == 0 and qb >= 1:
                                fin_dmas(s, qb - 1, qs)
                            pr = pairs[gl * 8 + qs]
                            gp = pr["gp"]
                            par3 = gp % 3
                            s.wait_ge(sm["expd"].h, gp + 1)
                            if pr["diag"]:
                                s.wait_ge(sm["msk"].h, pr["di"] + 1)
                            if gp >= 3:
                                s.wait_ge(sm["av"].h, gp - 2)
                            nk = pr["nksc"]
                            ins = s.dma_start(
                                pt_sb[:, par3 * KC:par3 * KC + nk * P]
                                .rearrange("p (k c) -> p k c", k=nk),
                                p_sb[:, par3 * KC:par3 * KC + nk * P]
                                .rearrange("p (k c) -> p k c", k=nk),
                                transpose=True)
                            st["pt"].inc(ins, gp)
                    if qb + 1 < NQB:
                        if qb >= 1:
                            s.wait_ge(sm["sc"].h, 8 * (gl_of(qb - 1, qb - 1) + 1))
                        ins = s.dma_start(
                            q_sb[:, ((qb + 1) % 2) * ND * QB:((qb + 1) % 2 + 1) * ND * QB]
                            .rearrange("p (c q) -> p c q", c=ND),
                            qfm.rearrange("(c p) q -> p c q", p=P)[
                                :, :, (qb + 1) * QB:(qb + 2) * QB])
                        st["qld"].inc(ins, qb + 1)
                        kv_load(qb + 1, 0)
                for qs in range(8):
                    fin_dmas(s, NQB - 1, qs)

            @block.tensor
            def _(t):
                def emit_av(pr):
                    gp = pr["gp"]
                    par = gp % 2
                    par3 = gp % 3
                    if pr["qs"] == 0:
                        st["vld"].wait_done(t, pr["gl"])
                    st["pt"].wait_done(t, gp)
                    if gp >= 2:
                        t.wait_ge(sm["oup"].h, gp - 1)
                    for dh in range(2):
                        for ksc in range(pr["nksc"]):
                            ins = t.matmul(
                                av_ps[par][:, dh * 512:(dh + 1) * 512],
                                pt_sb[:, par3 * KC + ksc * P:par3 * KC + (ksc + 1) * P],
                                vsl(pr["gl"], ksc, dh * 512, 512),
                                start=(ksc == 0), stop=(ksc == pr["nksc"] - 1))
                    sm["av"].inc(ins, 1)

                for pr in pairs:
                    gp = pr["gp"]
                    par = gp % 2
                    if pr["qs"] == 0:
                        st["kld"].wait_done(t, pr["gl"])
                        if pr["kc"] == 0:
                            st["qld"].wait_done(t, pr["qb"])
                    if gp >= 2:
                        t.wait_ge(sm["expd"].h, gp - 1)
                    for nh in range(pr["nh"]):
                        for dc in range(ND):
                            ins = t.matmul(
                                s_ps[par][:, nh * 512:(nh + 1) * 512],
                                qsl(pr["qb"], dc, pr["qs"]),
                                ksl(pr["gl"], dc, nh * 512, 512),
                                start=(dc == 0), stop=(dc == ND - 1))
                        if nh == 0 and pr["nh"] == 2:
                            sm["sch"].inc(ins, 1)
                    sm["sc"].inc(ins, 1)
                    if gp >= 2 and pairs[gp - 2]["qb"] == pr["qb"]:
                        emit_av(pairs[gp - 2])
                    if pr["kc"] == pr["qb"] and pr["qs"] == 7:
                        if gp >= 1 and pairs[gp - 1]["qb"] == pr["qb"]:
                            emit_av(pairs[gp - 1])
                        emit_av(pr)

            @block.gpsimd
            def _(g):
                chunks = []
                for e in range(8):
                    for h in range(2):
                        chunks.append((glp[:, e * 4096 + h * 2048:e * 4096 + (h + 1) * 2048],
                                       wg_d[e][:, h * 2048:(h + 1) * 2048]))
                        chunks.append((ulp[:, e * 4096 + h * 2048:e * 4096 + (h + 1) * 2048],
                                       wu_d[e][:, h * 2048:(h + 1) * 2048]))
                        chunks.append((dlp[e * 4096 + h * 2048:e * 4096 + (h + 1) * 2048, :],
                                       wd_d[e][h * 2048:(h + 1) * 2048, :]))
                NCH = len(chunks)
                assert 16 * NCH == WCAST_ALL
                n_p1 = 10
                items = []
                for i in range(NCH):
                    if i < n_p1:
                        items.append((-1000 + i, "cast", i))
                    else:
                        frac = (i - n_p1) / max(1, NCH - n_p1 - 1)
                        items.append((int((frac ** 1.3) * 0.8 * NPAIR), "cast", i))
                for pr in pairs:
                    if pr["diag"]:
                        items.append((pr["gp"] + 1, "mask", pr))
                items.sort(key=lambda x: (x[0], x[1] == "cast"))
                for key, kind, payload in items:
                    if kind == "cast":
                        i = payload
                        if i < n_p1:
                            g.wait_ge(sm["vmm"].h, min(8 * NG, 6 * i))
                        else:
                            g.wait_ge(sm["sc"].h, min(NPAIR, max(0, key)))
                        ins = g.dma_start(*chunks[i])
                        sm["wcast"].inc(ins, 16)
                    else:
                        pr = payload
                        gp, qs = pr["gp"], pr["qs"]
                        par3 = gp % 3
                        g.wait_ge(sm["expd"].h, gp + 1)
                        ins = g.affine_select(
                            p_sb[:, par3 * KC:par3 * KC + pr["W"]],
                            p_sb[:, par3 * KC:par3 * KC + pr["W"]],
                            [[-1, pr["W"]]], ALU.is_ge, 0.0,
                            base=qs * P, channel_multiplier=1)
                        sm["msk"].inc(ins, 1)

            def dve_finalize(v, qb, qs):
                fq = qb * 8 + qs
                par = fq % 2
                st["xres"].wait_done(v, fq)
                v.drain()
                v.reciprocal(sml[:, 38:39], l_sb[:, qs:qs + 1])
                v.drain()
                ins = v.scalar_tensor_tensor(
                    h_pre[:, :], o_acc[:, qs * D:(qs + 1) * D],
                    sml[:, 38:39], x_res[:, par * D:(par + 1) * D],
                    ALU.mult, ALU.add)
                sm["fdve"].inc(ins, 1)
                v.drain()
                v.scalar_tensor_tensor(
                    scr[:, :], h_pre[:, :], 1.0, h_pre[:, :],
                    ALU.mult, ALU.mult, accum_out=sml[:, 32:33])
                v.wait_ge(sm["fact"].h, 2 * fq + 1)
                v.tensor_scalar_mul(sml[:, 34:35], sml[:, 33:34], -1.0 / D)
                v.drain()
                v.tensor_tensor(sml[:, 35:36], sml[:, 34:35],
                                sml[:, 34:35], ALU.mult)
                v.drain()
                v.scalar_tensor_tensor(
                    sml[:, 35:36], sml[:, 32:33], 1.0 / D,
                    sml[:, 35:36], ALU.mult, ALU.subtract)
                v.drain()
                ins = v.tensor_scalar_add(sml[:, 35:36],
                                          sml[:, 35:36], EPS)
                sm["fdve"].inc(ins, 1)
                v.wait_ge(sm["fact"].h, 2 * fq + 2)
                if fq >= 2:
                    st["tokst"].wait_done(v, fq - 2)
                v.drain()
                ins = v.tensor_scalar(h1f32[:, par * D:(par + 1) * D],
                                      h_pre[:, :], sml[:, 34:35],
                                      sml[:, 36:37], ALU.add, ALU.mult)
                sm["fdve"].inc(ins, 1)
                if fq >= 2:
                    st["h1tr"].wait_done(v, fq - 2)
                v.drain()
                ins = v.tensor_copy(h1lp[:, par * D:(par + 1) * D],
                                    h1f32[:, par * D:(par + 1) * D])
                sm["fin"].inc(ins, 1)

            def cmax_part(v, pr):
                gp = pr["gp"]
                sb = (gp % 2) * 12
                first = pr["first"]
                if pr["nh"] == 2:
                    v.wait_ge(sm["sch"].h, cum_sch[gp + 1])
                    v.reduce_max(sml[:, sb:sb + 1],
                                 s_ps[gp % 2][:, 0:512], axis=AX)
                    v.wait_ge(sm["sc"].h, gp + 1)
                    v.reduce_max(sml[:, sb + 1:sb + 2],
                                 s_ps[gp % 2][:, 512:1024], axis=AX)
                    v.drain()
                    v.tensor_tensor(sml[:, sb:sb + 1], sml[:, sb:sb + 1],
                                    sml[:, sb + 1:sb + 2], ALU.max)
                else:
                    v.wait_ge(sm["sc"].h, gp + 1)
                    v.reduce_max(sml[:, sb:sb + 1],
                                 s_ps[gp % 2][:, 0:pr["W"]], axis=AX)
                v.drain()
                if first:
                    v.tensor_scalar_mul(sml[:, sb + 2:sb + 3],
                                        sml[:, sb:sb + 1], SCALE)
                else:
                    v.tensor_scalar(sml[:, sb + 2:sb + 3],
                                    sml[:, sb:sb + 1], SCALE,
                                    m_sb[:, pr["qs"]:pr["qs"] + 1],
                                    ALU.mult, ALU.max)
                v.drain()
                if not first:
                    v.tensor_tensor(sml[:, sb + 3:sb + 4],
                                    m_sb[:, pr["qs"]:pr["qs"] + 1],
                                    sml[:, sb + 2:sb + 3], ALU.subtract)
                v.tensor_scalar_mul(sml[:, sb + 4:sb + 5],
                                    sml[:, sb + 2:sb + 3], -1.0)
                v.drain()
                ins = v.tensor_copy(m_sb[:, pr["qs"]:pr["qs"] + 1],
                                    sml[:, sb + 2:sb + 3])
                sm["dve1"].inc(ins, 1)

            def o_part(v, pr):
                gp, qs = pr["gp"], pr["qs"]
                lc = 24 + (gp % 4) * 2
                first = pr["first"]
                if pr["diag"]:
                    v.wait_ge(sm["msk"].h, pr["di"] + 1)
                    v.drain()
                    v.reduce_sum(sml[:, lc:lc + 1],
                                 p_sb[:, (gp % 3) * KC:(gp % 3) * KC + pr["W"]],
                                 axis=AX)
                    v.drain()
                v.wait_ge(sm["av"].h, gp + 1)
                if first:
                    v.tensor_copy(l_sb[:, qs:qs + 1], sml[:, lc:lc + 1])
                    for dh in range(2):
                        ins = v.tensor_copy(
                            o_acc[:, qs * D + dh * 512:qs * D + (dh + 1) * 512],
                            av_ps[gp % 2][:, dh * 512:(dh + 1) * 512])
                else:
                    v.scalar_tensor_tensor(
                        l_sb[:, qs:qs + 1], l_sb[:, qs:qs + 1],
                        sml[:, lc + 1:lc + 2], sml[:, lc:lc + 1],
                        ALU.mult, ALU.add)
                    for dh in range(2):
                        ins = v.scalar_tensor_tensor(
                            o_acc[:, qs * D + dh * 512:qs * D + (dh + 1) * 512],
                            o_acc[:, qs * D + dh * 512:qs * D + (dh + 1) * 512],
                            sml[:, lc + 1:lc + 2],
                            av_ps[gp % 2][:, dh * 512:(dh + 1) * 512],
                            ALU.mult, ALU.add)
                sm["oup"].inc(ins, 1)

            @block.vector
            def _(v):
                for pr in pairs:
                    gp = pr["gp"]
                    if pr["kc"] == 0 and pr["qb"] >= 1:
                        dve_finalize(v, pr["qb"] - 1, pr["qs"])
                    cmax_part(v, pr)
                    if gp >= 2 and pairs[gp - 2]["qb"] == pr["qb"]:
                        o_part(v, pairs[gp - 2])
                    if pr["kc"] == pr["qb"] and pr["qs"] == 7:
                        if gp >= 1 and pairs[gp - 1]["qb"] == pr["qb"]:
                            o_part(v, pairs[gp - 1])
                        o_part(v, pr)
                for qs in range(8):
                    dve_finalize(v, NQB - 1, qs)

            def act_finalize(sc, qb, qs):
                fq = qb * 8 + qs
                sc.wait_ge(sm["fdve"].h, 3 * fq + 1)
                ins = sc.activation(scr_a[:, :], h_pre[:, :], AF.Copy,
                                    accum_out=sml[:, 33:34])
                sm["fact"].inc(ins, 1)
                sc.wait_ge(sm["fdve"].h, 3 * fq + 2)
                sc.activation(sml[:, 36:37], sml[:, 35:36], AF.Ln)
                sc.drain()
                ins = sc.activation(sml[:, 36:37], sml[:, 36:37],
                                    AF.Exp, scale=-0.5)
                sm["fact"].inc(ins, 1)

            @block.scalar
            def _(sc):
                for qb in range(NQB):
                    for kc in range(qb + 1):
                        gl = gl_of(qb, kc)
                        for qs in range(8):
                            if kc == 0 and qb >= 1:
                                act_finalize(sc, qb - 1, qs)
                            pr = pairs[gl * 8 + qs]
                            gp = pr["gp"]
                            sb = (gp % 2) * 12
                            sc.wait_ge(sm["dve1"].h, gp + 1)
                            lc = 24 + (gp % 4) * 2
                            if not pr["first"]:
                                sc.activation(sml[:, lc + 1:lc + 2],
                                              sml[:, sb + 3:sb + 4], AF.Exp)
                                sc.drain()
                            if gp >= 3:
                                st["pt"].wait_done(sc, gp - 3)
                                sc.wait_ge(sm["oup"].h, gp - 2)
                            ins = sc.activation(
                                p_sb[:, (gp % 3) * KC:(gp % 3) * KC + pr["W"]],
                                s_ps[gp % 2][:, 0:pr["W"]], AF.Exp,
                                bias=sml[:, sb + 4:sb + 5], scale=SCALE,
                                accum_out=(None if pr["diag"]
                                           else sml[:, lc:lc + 1]))
                            sm["expd"].inc(ins, 1)
                for qs in range(8):
                    act_finalize(sc, NQB - 1, qs)

        with ExitStack() as p4:
            e4 = p4.enter_context
            h1_sb = e4(nc.sbuf_tensor("h1_sb", [P, 2 * ND * TS_T], BF16))
            wg_sb = e4(nc.sbuf_tensor("wg_sb", [P, 3 * ND * FCB * P], BF16))
            wu_sb = e4(nc.sbuf_tensor("wu_sb", [P, 3 * ND * FCB * P], BF16))
            wd_sb = e4(nc.sbuf_tensor("wd_sb", [P, 3 * FCB * D], BF16))
            a_silu = e4(nc.sbuf_tensor("a_silu", [P, 2 * TS_T], BF16))
            a_sb = e4(nc.sbuf_tensor("a_sb", [P, 2 * TS_T], BF16))
            h1res = e4(nc.sbuf_tensor("h1res", [P, 2 * 3 * D], F32))
            s_pre = e4(nc.sbuf_tensor("s_pre", [P, 3 * D], F32))
            scr4 = e4(nc.sbuf_tensor("scr4", [P, D], F32))
            scr4a = e4(nc.sbuf_tensor("scr4a", [P, D], F32))
            out_sb = e4(nc.sbuf_tensor("out_sb", [P, 4 * D], F32))
            sml4 = e4(nc.sbuf_tensor("sml4", [P, 24], F32))
            g_ps = e4(nc.psum_tensor("g_ps", [P, TS_T], F32))
            u_ps = e4(nc.psum_tensor("u_ps", [P, TS_T], F32))
            y_ps = [e4(nc.psum_tensor(f"y_ps{i}", [P, 512], F32)) for i in range(6)]

            fq2_of_ts = []
            acc = 0
            for (t0v, T) in ts_list:
                fq2_of_ts.append(acc)
                acc += T // P

            def wg_slice(gfcb, dc, fcl):
                base = (gfcb % 3) * ND * FCB * P + dc * FCB * P
                return wg_sb[:, base + fcl * P:base + (fcl + 1) * P]

            def wu_slice(gfcb, dc, fcl):
                base = (gfcb % 3) * ND * FCB * P + dc * FCB * P
                return wu_sb[:, base + fcl * P:base + (fcl + 1) * P]

            def wd_slice(gfcb, fcl, dh):
                base = (gfcb % 3) * FCB * D + fcl * D
                return wd_sb[:, base + dh * 512:base + (dh + 1) * 512]

            def p4_stores(s, ts):
                t0v, T = ts_list[ts]
                for tc in range(T // P):
                    fq2 = fq2_of_ts[ts] + tc
                    s.wait_ge(sm["h2dve"].h, 3 * fq2 + 3)
                    ins = s.dma_start(
                        out_d[t0v + tc * P:t0v + (tc + 1) * P, :],
                        out_sb[:, (fq2 % 4) * D:(fq2 % 4 + 1) * D])
                    st["outst"].inc(ins, fq2)

            @block.sync
            def _(s):
                s.wait_ge(sm["wcast"].h, WCAST_ALL)
                st["h1st"].wait_all(s)
                for ts in range(NTS):
                    for fcb in range(NFCB):
                        if fcb == NFCB // 2 and ts >= 1:
                            p4_stores(s, ts - 1)
                        gfcb = ts * NFCB + fcb
                        if gfcb >= 3:
                            s.wait_ge(sm["gmm"].h, 4 * (gfcb - 2))
                        ins = s.dma_start(
                            wg_sb[:, (gfcb % 3) * ND * FCB * P:
                                  ((gfcb % 3) + 1) * ND * FCB * P]
                            .rearrange("p (c f) -> p c f", c=ND),
                            glp.rearrange("(c p) f -> p c f", p=P)[
                                :, :, fcb * FCB * P:(fcb + 1) * FCB * P])
                        st["wgld"].inc(ins, gfcb)
                        if gfcb >= 3:
                            s.wait_ge(sm["umm"].h, 4 * (gfcb - 2))
                        ins = s.dma_start(
                            wu_sb[:, (gfcb % 3) * ND * FCB * P:
                                  ((gfcb % 3) + 1) * ND * FCB * P]
                            .rearrange("p (c f) -> p c f", c=ND),
                            ulp.rearrange("(c p) f -> p c f", p=P)[
                                :, :, fcb * FCB * P:(fcb + 1) * FCB * P])
                        st["wuld"].inc(ins, gfcb)
                p4_stores(s, NTS - 1)
                st["outst"].wait_all(s)

            def h1_res_load(g, ts):
                t0v, T = ts_list[ts]
                if ts >= 2:
                    g.wait_ge(sm["umm"].h, NFC * (ts - 1))
                ins = g.dma_start(
                    h1_sb[:, (ts % 2) * ND * TS_T:(ts % 2) * ND * TS_T + ND * T]
                    .rearrange("p (c t) -> p c t", c=ND),
                    h1fm.rearrange("(c p) t -> p c t", p=P)[:, :, t0v:t0v + T])
                st["h1ld"].inc(ins, ts)
                if ts >= 2:
                    g.wait_ge(sm["h2ev"].h, ts - 1)
                ins = g.dma_start(
                    h1res[:, (ts % 2) * 3 * D:(ts % 2) * 3 * D + (T // P) * D]
                    .rearrange("p (tc d) -> p tc d", tc=T // P),
                    h1tok.rearrange("(tc p) d -> tc p d", p=P)[
                        t0v // P:t0v // P + T // P].rearrange("tc p d -> p tc d"))
                st["resld"].inc(ins, ts)

            @block.gpsimd
            def _(g):
                g.wait_ge(sm["wcast"].h, WCAST_ALL)
                st["tokst"].wait_all(g)
                st["h1st"].wait_all(g)
                h1_res_load(g, 0)
                for ts in range(NTS):
                    t0v, T = ts_list[ts]
                    if ts + 1 < NTS:
                        h1_res_load(g, ts + 1)
                    for fcb in range(NFCB):
                        gfcb = ts * NFCB + fcb
                        if gfcb >= 3:
                            g.wait_ge(sm["ydone"].h, 4 * (gfcb - 2))
                        ins = g.dma_start(
                            wd_sb[:, (gfcb % 3) * FCB * D:((gfcb % 3) + 1) * FCB * D]
                            .rearrange("p (c d) -> p c d", c=FCB),
                            dlp.rearrange("(c p) d -> p c d", p=P)[
                                :, fcb * FCB:(fcb + 1) * FCB, :])
                        st["wdld"].inc(ins, gfcb)

            @block.tensor
            def _(t):
                def h1sl(ts, dc, T):
                    base = (ts % 2) * ND * TS_T + dc * T
                    return h1_sb[:, base:base + T]

                def y_mms(ts, fc, T, start, stop):
                    apar = (ts * NFC + fc) % 2
                    gfcb = ts * NFCB + fc // FCB
                    for tc in range(T // P):
                        for dh in range(2):
                            ins = t.matmul(
                                y_ps[tc * 2 + dh][:, :],
                                a_sb[:, apar * TS_T + tc * P:apar * TS_T + (tc + 1) * P],
                                wd_slice(gfcb, fc % FCB, dh),
                                start=start, stop=stop)
                    return ins

                for ts in range(NTS):
                    t0v, T = ts_list[ts]
                    st["h1ld"].wait_done(t, ts)
                    for fc in range(NFC):
                        gfc = ts * NFC + fc
                        gfcb = ts * NFCB + fc // FCB
                        if fc % FCB == 0:
                            st["wgld"].wait_done(t, gfcb)
                            st["wuld"].wait_done(t, gfcb)
                        if gfc >= 1:
                            t.wait_ge(sm["silu"].h, gfc)
                        for dc in range(ND):
                            ins = t.matmul(g_ps[:, 0:T],
                                           wg_slice(gfcb, dc, fc % FCB),
                                           h1sl(ts, dc, T),
                                           start=(dc == 0), stop=(dc == ND - 1))
                        sm["gmm"].inc(ins, 1)
                        if gfc >= 1:
                            t.wait_ge(sm["mul"].h, gfc)
                        for dc in range(ND):
                            ins = t.matmul(u_ps[:, 0:T],
                                           wu_slice(gfcb, dc, fc % FCB),
                                           h1sl(ts, dc, T),
                                           start=(dc == 0), stop=(dc == ND - 1))
                        sm["umm"].inc(ins, 1)
                        if fc >= 1:
                            fp = fc - 1
                            if fp % FCB == 0:
                                st["wdld"].wait_done(t, ts * NFCB + fp // FCB)
                            t.wait_ge(sm["mul"].h, gfc)
                            if fc == 1 and ts >= 1:
                                t.wait_ge(sm["h2ev"].h, ts)
                            ins = y_mms(ts, fp, T, start=(fp == 0), stop=False)
                            sm["ydone"].inc(ins, 1)
                    t.wait_ge(sm["mul"].h, ts * NFC + NFC)
                    ins = y_mms(ts, NFC - 1, T, start=False, stop=True)
                    sm["ydone"].inc(ins, 1)

            def ln2_lsum(sc, ts, tc):
                fq2 = fq2_of_ts[ts] + tc
                q = (fq2 % 3) * 8
                sc.wait_ge(sm["h2dve"].h, 3 * fq2 + 1)
                ins = sc.activation(scr4a[:, :],
                                    s_pre[:, (fq2 % 3) * D:(fq2 % 3 + 1) * D],
                                    AF.Copy, accum_out=sml4[:, q + 1:q + 2])
                sm["h2act"].inc(ins, 1)

            def ln2_rstd(sc, ts, tc):
                fq2 = fq2_of_ts[ts] + tc
                q = (fq2 % 3) * 8
                sc.wait_ge(sm["h2dve"].h, 3 * fq2 + 2)
                sc.activation(sml4[:, q + 4:q + 5], sml4[:, q + 3:q + 4], AF.Ln)
                sc.drain()
                ins = sc.activation(sml4[:, q + 4:q + 5], sml4[:, q + 4:q + 5],
                                    AF.Exp, scale=-0.5)
                sm["h2act"].inc(ins, 1)

            @block.scalar
            def _(sc):
                for ts in range(NTS):
                    t0v, T = ts_list[ts]
                    hooks = {}
                    if ts >= 1:
                        pT = ts_list[ts - 1][1]
                        for tc in range(pT // P):
                            hooks[14 + 40 * tc] = (ln2_lsum, tc)
                            hooks[30 + 40 * tc] = (ln2_rstd, tc)
                    for fc in range(NFC):
                        gfc = ts * NFC + fc
                        par = gfc % 2
                        sc.wait_ge(sm["gmm"].h, gfc + 1)
                        if gfc >= 2:
                            sc.wait_ge(sm["mul"].h, gfc - 1)
                        ins = sc.activation(a_silu[:, par * TS_T:par * TS_T + T],
                                            g_ps[:, 0:T], AF.Silu)
                        sm["silu"].inc(ins, 1)
                        if fc in hooks:
                            fn, tc = hooks[fc]
                            fn(sc, ts - 1, tc)
                for tc in range(ts_list[NTS - 1][1] // P):
                    ln2_lsum(sc, NTS - 1, tc)
                    ln2_rstd(sc, NTS - 1, tc)

            def ln2_adds(v, ts, tc):
                fq2 = fq2_of_ts[ts] + tc
                sp = (fq2 % 3) * D
                rpar = ts % 2
                T = ts_list[ts][1]
                if tc == 0:
                    v.wait_ge(sm["ydone"].h, NFC * (ts + 1))
                    st["resld"].wait_done(v, ts)
                v.drain()
                for dh in range(2):
                    ins = v.tensor_tensor(
                        s_pre[:, sp + dh * 512:sp + (dh + 1) * 512],
                        y_ps[tc * 2 + dh][:, :],
                        h1res[:, rpar * 3 * D + tc * D + dh * 512:
                              rpar * 3 * D + tc * D + (dh + 1) * 512],
                        ALU.add)
                if tc == T // P - 1:
                    sm["h2ev"].inc(ins, 1)

            def ln2_ssq(v, ts, tc):
                fq2 = fq2_of_ts[ts] + tc
                sp = (fq2 % 3) * D
                q = (fq2 % 3) * 8
                v.drain()
                ins = v.scalar_tensor_tensor(
                    scr4[:, :], s_pre[:, sp:sp + D], 1.0, s_pre[:, sp:sp + D],
                    ALU.mult, ALU.mult, accum_out=sml4[:, q:q + 1])
                sm["h2dve"].inc(ins, 1)

            def ln2_meanvar(v, ts, tc):
                fq2 = fq2_of_ts[ts] + tc
                q = (fq2 % 3) * 8
                v.wait_ge(sm["h2act"].h, 2 * fq2 + 1)
                v.drain()
                v.tensor_scalar_mul(sml4[:, q + 2:q + 3], sml4[:, q + 1:q + 2],
                                    -1.0 / D)
                v.drain()
                v.tensor_tensor(sml4[:, q + 3:q + 4], sml4[:, q + 2:q + 3],
                                sml4[:, q + 2:q + 3], ALU.mult)
                v.drain()
                v.scalar_tensor_tensor(
                    sml4[:, q + 3:q + 4], sml4[:, q:q + 1], 1.0 / D,
                    sml4[:, q + 3:q + 4], ALU.mult, ALU.subtract)
                v.drain()
                ins = v.tensor_scalar_add(sml4[:, q + 3:q + 4],
                                          sml4[:, q + 3:q + 4], EPS)
                sm["h2dve"].inc(ins, 1)

            def ln2_apply(v, ts, tc):
                fq2 = fq2_of_ts[ts] + tc
                sp = (fq2 % 3) * D
                q = (fq2 % 3) * 8
                v.wait_ge(sm["h2act"].h, 2 * fq2 + 2)
                if fq2 >= 4:
                    st["outst"].wait_done(v, fq2 - 4)
                v.drain()
                ins = v.tensor_scalar(
                    out_sb[:, (fq2 % 4) * D:(fq2 % 4 + 1) * D],
                    s_pre[:, sp:sp + D], sml4[:, q + 2:q + 3],
                    sml4[:, q + 4:q + 5], ALU.add, ALU.mult)
                sm["h2dve"].inc(ins, 1)

            @block.vector
            def _(v):
                for ts in range(NTS):
                    t0v, T = ts_list[ts]
                    hooks = {}
                    if ts >= 1:
                        pT = ts_list[ts - 1][1]
                        for tc in range(pT // P):
                            ln2_adds(v, ts - 1, tc)
                            hooks[8 + 40 * tc] = (ln2_ssq, tc)
                            hooks[24 + 40 * tc] = (ln2_meanvar, tc)
                            hooks[40 + 40 * tc] = (ln2_apply, tc)
                    for fc in range(NFC):
                        gfc = ts * NFC + fc
                        par = gfc % 2
                        v.wait_ge(sm["silu"].h, gfc + 1)
                        v.wait_ge(sm["umm"].h, gfc + 1)
                        if gfc >= 2:
                            v.wait_ge(sm["ydone"].h, gfc - 1)
                        ins = v.tensor_tensor(a_sb[:, par * TS_T:par * TS_T + T],
                                              a_silu[:, par * TS_T:par * TS_T + T],
                                              u_ps[:, 0:T], ALU.mult)
                        sm["mul"].inc(ins, 1)
                        if fc in hooks:
                            fn, tc = hooks[fc]
                            fn(v, ts - 1, tc)
                for tc in range(ts_list[NTS - 1][1] // P):
                    ln2_adds(v, NTS - 1, tc)
                    ln2_ssq(v, NTS - 1, tc)
                    ln2_meanvar(v, NTS - 1, tc)
                    ln2_apply(v, NTS - 1, tc)

    return nc


def kernel(L0_attn_x, L0_attn_Qw, L0_attn_Kw, L0_attn_Vw,
           L0_ffn_Wg, L0_ffn_Wu, L0_ffn_Wd):
    x = np.ascontiguousarray(np.asarray(L0_attn_x, dtype=np.float32))
    qw = np.ascontiguousarray(np.asarray(L0_attn_Qw, dtype=np.float32))
    kw = np.ascontiguousarray(np.asarray(L0_attn_Kw, dtype=np.float32))
    vw = np.ascontiguousarray(np.asarray(L0_attn_Vw, dtype=np.float32))
    wg = np.ascontiguousarray(np.asarray(L0_ffn_Wg, dtype=np.float32))
    wu = np.ascontiguousarray(np.asarray(L0_ffn_Wu, dtype=np.float32))
    wd = np.ascontiguousarray(np.asarray(L0_ffn_Wd, dtype=np.float32))
    B, S, _ = x.shape
    nc = build(S)
    in_maps = [dict(x=x[b], qw=qw, kw=kw, vw=vw, wg=wg, wu=wu, wd=wd)
               for b in range(B)]
    res = run_bass_kernel_spmd(nc, in_maps, core_ids=list(range(B)))
    return np.stack([r["out"] for r in res.results], axis=0)
